# version 19
# baseline (speedup 1.0000x reference)
import numpy as np
import ml_dtypes
import concourse.bass as bass
import concourse.mybir as mybir
from concourse.bass_utils import run_bass_kernel_spmd
from contextlib import ExitStack

F32 = mybir.dt.float32
BF16 = mybir.dt.bfloat16
AF = mybir.ActivationFunctionType
OP = mybir.AluOpType
AX = mybir.AxisListType.X

NCORES = 8
D = 1024
DB = 8
T = 2048
HL = 3
N = 16
HD = 4096
JT = HD // 128
TC = 512
NTC = T // TC
TF = 256
NTF = T // TF
LNEPS = 1e-5
ABSEPS = 1e-6


class Rec:

    NQ = 6
    QUEUES = ("QS", "QA", "QG")
    ENG = ("V", "S", "T", "P", "CC") + tuple(
        f"{q}{j}" for q in ("QS", "QA", "QG") for j in range(6))
    ISSUE = {"V": "v", "S": "s", "T": "t", "P": "p", "CC": "p"}
    for _q, _st in (("QS", "sync"), ("QA", "s"), ("QG", "p")):
        for _j in range(6):
            ISSUE[f"{_q}{_j}"] = _st

    def __init__(self, dma_serial=False):
        self.dma_serial = dma_serial
        self.items = {k: [] for k in ("v", "s", "t", "p", "sync")}
        self.cnt = {e: 0 for e in self.ENG}
        self.qrot = {q: 0 for q in self.QUEUES}
        self.writer = {}
        self.readers = {}
        self.covered = {i: {e: 0 for e in self.ENG} for i in self.items}

    SELF_FENCE = ("V", "S", "P")

    def emit(self, eng, fn, reads=(), writes=(), inc=True, fence=False):
        qthrottle = None
        if eng in self.QUEUES:
            j = self.qrot[eng]
            self.qrot[eng] = (j + 1) % self.NQ
            eng = f"{eng}{j}"
            qthrottle = eng
        stream = self.ISSUE[eng]
        deps = {}
        self_fence = eng in self.SELF_FENCE

        def need(w):
            if w is None:
                return
            weng, wcnt, wfen = w
            if weng != eng or self_fence or wfen:
                deps[weng] = max(deps.get(weng, 0), wcnt)

        for k in reads:
            need(self.writer.get(k))
        for k in writes:
            need(self.writer.get(k))
            for reng, rcnt in self.readers.get(k, {}).items():
                if reng != eng or self_fence:
                    deps[reng] = max(deps.get(reng, 0), rcnt)
        if qthrottle is not None or (self.dma_serial and (eng[:2] in self.QUEUES or eng == "CC")):
            if self.dma_serial:
                for q2 in self.ENG:
                    if q2[:2] in self.QUEUES or q2 == "CC":
                        deps[q2] = max(deps.get(q2, 0), self.cnt[q2])
            elif qthrottle is not None:
                deps[qthrottle] = max(deps.get(qthrottle, 0), self.cnt[qthrottle])

        if eng in deps and deps[eng] > self.cnt[eng]:
            deps[eng] = self.cnt[eng]
        for other, cval in sorted(deps.items()):
            if other == eng and cval <= self.covered[stream].get(f"_self_{eng}", 0):
                continue
            if self.covered[stream][other] < cval or other == eng:
                if other == eng:
                    if cval <= self.covered[stream].get(f"_self_{eng}", 0):
                        continue
                    self.items[stream].append(("wait", other, cval))
                    self.covered[stream][f"_self_{eng}"] = cval
                else:
                    self.items[stream].append(("wait", other, cval))
                    self.covered[stream][other] = cval
        step = 16 if eng[:2] in self.QUEUES else 1
        if inc:
            self.items[stream].append(("inst", fn, eng, step))
            self.cnt[eng] += step
            mark = self.cnt[eng]
        else:
            self.items[stream].append(("inst", fn, None, 0))
            mark = self.cnt[eng] + step
        for k in reads:
            self.readers.setdefault(k, {})[eng] = mark
        for k in writes:
            self.writer[k] = (eng, mark, fence)
            self.readers[k] = {}

    def scope(self, name):
        for stream in self.items:
            self.items[stream].append(("scope", name))

    def full_barrier(self):
        for stream in self.items:
            for e in self.ENG:
                c = self.cnt[e]
                if c and self.covered[stream].get(e, 0) < c:
                    self.items[stream].append(("wait", e, c))
                    self.covered[stream][e] = c

    def drain(self, eng, on_stream):
        engs = [f"{eng}{j}" for j in range(self.NQ)] if eng in self.QUEUES else [eng]
        for e in engs:
            c = self.cnt[e]
            if c and self.covered[on_stream][e] < c:
                self.items[on_stream].append(("wait", e, c))
                self.covered[on_stream][e] = c


def build(dbg=(), stop_after="full", p1_nchunks=NTC, dma_serial=False):
    dbg = set(dbg)
    nc = bass.Bass("TRN2", target_bir_lowering=False, debug=False)

    def dram_in(name, shape, dt=F32):
        return nc.dram_tensor(name, shape, dt, kind="ExternalInput").ap()

    x_sl = dram_in("x_sl", [D, T + HL])
    w_in_t = dram_in("w_in_t", [D, 2 * D])
    w_x_t = dram_in("w_x_t", [D, 2 * N])
    w_dt_t = dram_in("w_dt_t", [N, D])
    w_out_t = dram_in("w_out_t", [D, D])
    w_fc_t = dram_in("w_fc_t", [D, HD])
    w_proj_t = dram_in("w_proj_t", [HD, D])
    conv_w2 = dram_in("conv_w2", [D, 4])
    b_dt2 = dram_in("b_dt2", [D, 1])
    a_log = dram_in("a_log", [D, N])
    lnw_in = dram_in("lnw_in", [HD, 1], F32)
    lnb_in = dram_in("lnb_in", [HD, 1], BF16)
    eye_in = dram_in("eye_in", [128, 128], BF16)
    onescb_in = dram_in("onescb_in", [128, 1], BF16)
    onesrb_in = dram_in("onesrb_in", [1, 128], BF16)
    onescf_in = dram_in("onescf_in", [128, 1], F32)
    onesrf_in = dram_in("onesrf_in", [1, 128], F32)
    maskprev = dram_in("maskprev", [128, NCORES])
    cns_in = dram_in("cns_in", [1, 2])
    gfc_sl = dram_in("gfc_sl", [128, HD])
    gpr_sl = dram_in("gpr_sl", [512, D])

    out_sl = nc.dram_tensor("out_sl", [D, T], F32, kind="ExternalOutput").ap()
    dbg_out = {}

    def tap(name, shape, dt=F32):
        if name in dbg:
            dbg_out[name] = nc.dram_tensor("tap_" + name, shape, dt,
                                           kind="ExternalOutput").ap()
        return dbg_out.get(name)

    sz_dram = nc.dram_tensor("sz_dram", [D, T], BF16, kind="Internal").ap()
    bmat_dram = nc.dram_tensor("bmat_dram", [N, T], BF16, kind="Internal").ap()
    yl_dram = nc.dram_tensor("yl_dram", [D, T], BF16, kind="Internal").ap()
    x1_dram = nc.dram_tensor("x1_dram", [D, T], BF16, kind="Internal").ap()
    cc_send_t = nc.dram_tensor("cc_send", [256, 128], F32)
    cc_recv_t = nc.dram_tensor("cc_recv", [NCORES * 256, 128], F32)
    gam_send_t = nc.dram_tensor("gam_send", [1, 2], F32)
    gam_recv_t = nc.dram_tensor("gam_recv", [1, 2], F32)

    rec = Rec(dma_serial=dma_serial)
    E = rec.emit

    def dma(q, out, in_, reads, writes, inc=True):
        E(q, lambda e: e.dma_start(out=out, in_=in_), reads=reads, writes=writes, inc=inc)

    with ExitStack() as octx:
        def sbuf(ctx, name, shape, dt):
            return ctx.enter_context(nc.sbuf_tensor(name, shape, dt))

        o = octx
        mo_t = [sbuf(o, f"mo{k}", [128, TC], BF16) for k in range(DB)]
        x2r = [sbuf(o, f"x2r{i}", [128, TC], BF16) for i in range(2)]
        rstbc = sbuf(o, "rstbc", [128, 2 * TC], BF16)
        srf = sbuf(o, "srf", [1, 1536], F32)
        srb = sbuf(o, "srb", [1, 1024], BF16)
        outf = sbuf(o, "outf", [128, TF], F32)
        tmp_bf = [sbuf(o, f"tmpbf{i}", [128, TC], BF16) for i in range(2)]
        g2r = [sbuf(o, f"g2r{i}", [128, TF], BF16) for i in range(4)]
        onescb = sbuf(o, "onescb", [128, 1], BF16)
        onesrb = sbuf(o, "onesrb", [1, 128], BF16)
        onescf = sbuf(o, "onescf", [128, 1], F32)
        onesrf = sbuf(o, "onesrf", [1, 128], F32)
        msk = sbuf(o, "msk", [128, NCORES], F32)
        cns = sbuf(o, "cns", [1, 2], F32)
        cw = sbuf(o, "cw", [128, 4 * DB], F32)
        bdt = sbuf(o, "bdt", [128, DB], F32)
        A_sb = [sbuf(o, f"A{k}", [128, N], F32) for k in range(DB)]
        lnw_sb = sbuf(o, "lnw_sb", [128, JT], BF16)
        lnwf_sb = sbuf(o, "lnwf_sb", [128, JT], F32)
        lnbc = [sbuf(o, f"lnbc{j}", [128, 1], BF16) for j in range(JT)]
        gacc = sbuf(o, "gacc", [128, 2], F32)
        gpart = sbuf(o, "gpart", [128, 1], F32)
        gam = sbuf(o, "gam", [1, 2], F32)
        thr = sbuf(o, "thr", [128, 2], F32)
        nthr = sbuf(o, "nthr", [128, 2], F32)
        wsum_bf = sbuf(o, "wsum_bf", [1, D], BF16)
        wlnb_f = sbuf(o, "wlnb_f", [128, DB], F32)
        negm_bf = sbuf(o, "negm_bf", [1, TF], BF16)
        wxdt = [sbuf(o, f"wxdt{k}", [128, N], BF16) for k in range(DB)]
        wxB = [sbuf(o, f"wxB{k}", [128, N], BF16) for k in range(DB)]

        psA = octx.enter_context(nc.psum_tensor("psA", [128, 2048], F32))
        psB = octx.enter_context(nc.psum_tensor("psB", [128, 2048], F32))

        SRM, SRV, SRMR = 0, 512, 1024
        SBR, SBMR = 0, 512

        def emit_ln(tiles, w, x2eng="V", dn=None):
            nt = len(tiles)
            Dn = float(dn if dn is not None else nt * 128)
            for i, (apf, rk) in enumerate(tiles):
                E("T", lambda e, apf=apf, i=i: e.matmul(
                    out=psB[0:1, 1536:1536 + w], lhsT=onescb[:], rhs=apf(),
                    start=(i == 0), stop=(i == nt - 1)),
                  reads=[rk, "onescb"], writes=["psB3"], inc=(i == nt - 1))
            for i, (apf, rk) in enumerate(tiles):
                x2 = x2r[i % 2]
                E(x2eng, lambda e, apf=apf, x2=x2: e.tensor_tensor(
                    out=x2[:, 0:w], in0=apf(), in1=apf(), op=OP.mult),
                  reads=[rk], writes=[f"x2_{i % 2}"])
                E("T", lambda e, i=i, x2=x2: e.matmul(
                    out=psB[0:1, 1536 - w:1536], lhsT=onescb[:], rhs=x2[:, 0:w],
                    start=(i == 0), stop=(i == nt - 1)),
                  reads=[f"x2_{i % 2}", "onescb"], writes=["psB2"], inc=True)
            E("V", lambda e: e.tensor_scalar(out=srf[:, 0:2 * w],
                                             in0=psB[0:1, 1536 - w:1536 + w],
                                             scalar1=1.0 / Dn, scalar2=None, op0=OP.mult),
              reads=["psB3", "psB2"], writes=["sm", "sv"])
            E("V", lambda e: e.tensor_tensor(out=srf[:, 2 * w:3 * w], in0=srf[:, w:2 * w],
                                             in1=srf[:, w:2 * w], op=OP.mult),
              reads=["sm"], writes=["smr"])
            E("V", lambda e: e.tensor_tensor(out=srf[:, 0:w], in0=srf[:, 0:w],
                                             in1=srf[:, 2 * w:3 * w], op=OP.subtract),
              reads=["smr", "sv"], writes=["sv"])
            E("S", lambda e: e.activation(out=srf[:, 0:w], in_=srf[:, 0:w],
                                          func=AF.Ln, bias=cns[0:1, 0:1]),
              reads=["sv", "cns"], writes=["sv"])
            E("S", lambda e: e.activation(out=srb[:, 0:w], in_=srf[:, 0:w],
                                          func=AF.Exp, scale=cns[0:1, 1:2]),
              reads=["sv", "cns"], writes=["srbr"])
            E("V", lambda e: e.tensor_tensor(out=srb[:, w:2 * w], in0=srf[:, w:2 * w],
                                             in1=srb[:, 0:w], op=OP.mult),
              reads=["sm", "srbr"], writes=["srbmr"])
            if 2 * w <= 512:
                E("T", lambda e: e.matmul(out=psB[:, 1024:1024 + 2 * w], lhsT=onesrb[:],
                                          rhs=srb[:, 0:2 * w], start=True, stop=True),
                  reads=["srbr", "srbmr", "onesrb"], writes=["psB2", "psB3"])
            else:
                E("T", lambda e: e.matmul(out=psB[:, 1024:1024 + w], lhsT=onesrb[:],
                                          rhs=srb[:, 0:w], start=True, stop=True),
                  reads=["srbr", "onesrb"], writes=["psB2"])
                E("T", lambda e: e.matmul(out=psB[:, 1024 + w:1024 + 2 * w], lhsT=onesrb[:],
                                          rhs=srb[:, w:2 * w], start=True, stop=True),
                  reads=["srbmr", "onesrb"], writes=["psB3"])
            E("S", lambda e: e.copy(out=rstbc[:, 0:2 * w], in_=psB[:, 1024:1024 + 2 * w]),
              reads=["psB2", "psB3"], writes=["rstbc0", "rstbc1"])
            return "rstbc0", "rstbc1"

        rec.scope("P0")
        dma("QG", onescb[:], onescb_in, ["EXT"], ["onescb"])
        dma("QG", onesrb[:], onesrb_in, ["EXT"], ["onesrb"])
        dma("QG", onescf[:], onescf_in, ["EXT"], ["onescf"])
        dma("QG", onesrf[:], onesrf_in, ["EXT"], ["onesrf"])
        dma("QG", msk[:], maskprev, ["EXT"], ["msk"])
        dma("QG", cns[:], cns_in, ["EXT"], ["cns"])
        for k in range(DB):
            dma("QG", cw[:, 4 * k:4 * k + 4], conv_w2[k * 128:(k + 1) * 128, :], ["EXT"], ["cw"])
            dma("QG", bdt[:, k:k + 1], b_dt2[k * 128:(k + 1) * 128, :], ["EXT"], ["bdt"])
            dma("QG", A_sb[k][:], a_log[k * 128:(k + 1) * 128, :], ["EXT"], [f"A{k}"])
            E("S", lambda e, k=k: e.activation(out=A_sb[k][:], in_=A_sb[k][:], func=AF.Exp),
              reads=[f"A{k}"], writes=[f"A{k}"])
            E("V", lambda e, k=k: e.tensor_scalar(out=A_sb[k][:], in0=A_sb[k][:],
                                                  scalar1=-1.0, scalar2=None, op0=OP.mult),
              reads=[f"A{k}"], writes=[f"A{k}"])
        for j in range(JT):
            dma("QG", lnwf_sb[:, j:j + 1], lnw_in[j * 128:(j + 1) * 128, :], ["EXT"], ["lnwf"])
            dma("QG", lnbc[j][:], lnb_in[j * 128:(j + 1) * 128, :], ["EXT"], [f"lnb{j}"])
        E("S", lambda e: e.copy(out=lnw_sb[:], in_=lnwf_sb[:]), reads=["lnwf"], writes=["lnw"])

        with ExitStack() as actx:
            a = actx
            xsb = [sbuf(a, f"xsb{k}", [128, T + HL], BF16) for k in range(DB)]
            szr = [sbuf(a, f"szr{i}", [128, TC], BF16) for i in range(2)]
            cv = [sbuf(a, f"cv{i}", [128, TC], BF16) for i in range(3)]
            xcar = [sbuf(a, f"xca{k}", [128, T], BF16) for k in range(DB)]
            eye_bf = sbuf(a, "eye_bf", [128, 128], BF16)
            hend = sbuf(a, "hend", [128, 128], F32)
            aprod = sbuf(a, "aprod", [128, 128], F32)
            ap16 = sbuf(a, "ap16", [128, N], F32)
            hrec = [sbuf(a, f"hrec{r}", [128, 128], F32) for r in range(2)]
            h0 = sbuf(a, "h0", [128, 128], F32)
            hP = sbuf(a, "hP", [128, 128], F32)
            wdtb = [sbuf(a, f"wdtb{k}", [N, 128], BF16) for k in range(DB)]
            dtlow = sbuf(a, "dtlow", [N, T], BF16)
            bmr = [sbuf(a, f"bmr{i}", [N, TC], BF16) for i in range(2)]
            dma("QG", eye_bf[:], eye_in, ["EXT"], ["eye"])

            stage2 = [sbuf(a, f"stage{i}", [128, T + HL], F32) for i in range(2)]
            def stg(i):
                return stage2[i % 2]
            si = [0]
            SQ = ("QS", "QA")
            for k in range(DB):
                i = si[0] % 2; si[0] += 1
                dma(SQ[i], stg(i)[:, 0:T + HL], x_sl[k * 128:(k + 1) * 128, :], ["EXT"], [f"stg{i}"])
                E("S", lambda e, k=k, i=i: e.copy(out=xsb[k][:], in_=stg(i)[:, 0:T + HL]),
                  reads=[f"stg{i}"], writes=[f"x{k}"])
            for k in range(DB):
                i = si[0] % 2; si[0] += 1
                dma(SQ[i], stg(i)[:, 0:2 * N], w_x_t[k * 128:(k + 1) * 128, :], ["EXT"], [f"stg{i}"])
                E("S", lambda e, k=k, i=i: e.copy(out=wxdt[k][:], in_=stg(i)[:, 0:N]),
                  reads=[f"stg{i}"], writes=[f"wxdt{k}"])
                E("S", lambda e, k=k, i=i: e.copy(out=wxB[k][:], in_=stg(i)[:, N:2 * N]),
                  reads=[f"stg{i}"], writes=[f"wxB{k}"])
            for k in range(DB):
                i = si[0] % 2; si[0] += 1
                dma(SQ[i], stg(i)[0:N, 0:128], w_dt_t[:, k * 128:(k + 1) * 128], ["EXT"], [f"stg{i}"])
                E("S", lambda e, k=k, i=i: e.copy(out=wdtb[k][:], in_=stg(i)[0:N, 0:128]),
                  reads=[f"stg{i}"], writes=[f"wdtb{k}"])

            gstage = sbuf(a, "gstage", [128, 512], F32)

            def emit_gamma():
                for i in range(8):
                    dma("QG", gstage[:], gfc_sl[:, i * 512:(i + 1) * 512], ["EXT"], ["gstg"])
                    E("V", lambda e: e.tensor_reduce(out=gpart[:], in_=gstage[:], axis=AX,
                                                     op=OP.add, apply_absolute_value=True),
                      reads=["gstg"], writes=["gpart"], fence=True)
                    if i == 0:
                        E("V", lambda e: e.tensor_copy(out=gacc[:, 0:1], in_=gpart[:]),
                          reads=["gpart"], writes=["gacc"])
                    else:
                        E("V", lambda e: e.tensor_tensor(out=gacc[:, 0:1], in0=gacc[:, 0:1],
                                                         in1=gpart[:], op=OP.add),
                          reads=["gpart", "gacc"], writes=["gacc"])
                for i in range(8):
                    dma("QG", gstage[:], gpr_sl[(i // 2) * 128:(i // 2 + 1) * 128, (i % 2) * 512:(i % 2 + 1) * 512], ["EXT"], ["gstg"])
                    E("V", lambda e: e.tensor_reduce(out=gpart[:], in_=gstage[:], axis=AX,
                                                     op=OP.add, apply_absolute_value=True),
                      reads=["gstg"], writes=["gpart"], fence=True)
                    if i == 0:
                        E("V", lambda e: e.tensor_copy(out=gacc[:, 1:2], in_=gpart[:]),
                          reads=["gpart", "gacc"], writes=["gacc"])
                    else:
                        E("V", lambda e: e.tensor_tensor(out=gacc[:, 1:2], in0=gacc[:, 1:2],
                                                         in1=gpart[:], op=OP.add),
                          reads=["gpart", "gacc"], writes=["gacc"])
                E("T", lambda e: e.matmul(out=psB[0:1, 0:2], lhsT=onescf[:], rhs=gacc[:],
                                          start=True, stop=True),
                  reads=["gacc", "onescf"], writes=["psB0"])
                E("V", lambda e: e.tensor_copy(out=gam[:], in_=psB[0:1, 0:2]),
                  reads=["psB0"], writes=["gam"])
                dma("QA", gam_send_t.ap()[:], gam[:], ["gam"], ["gamd"])
                E("CC", lambda e: e.collective_compute(
                    "AllReduce", OP.add, replica_groups=[list(range(NCORES))],
                    ins=[gam_send_t.ap().opt()], outs=[gam_recv_t.ap().opt()]),
                  reads=["gamd"], writes=["gamr"])

            def emit_gamma_thr():
                dma("QS", gam[:], gam_recv_t.ap()[:], ["gamr"], ["gam"])
                E("V", lambda e: e.tensor_scalar(out=gam[:], in0=gam[:],
                                                 scalar1=0.5 / (D * HD), scalar2=0.5 * ABSEPS,
                                                 op0=OP.mult, op1=OP.add),
                  reads=["gam"], writes=["gam"])
                E("T", lambda e: e.matmul(out=psB[:, 4:6], lhsT=onesrf[:], rhs=gam[:],
                                          start=True, stop=True),
                  reads=["gam", "onesrf"], writes=["psB0"])
                E("V", lambda e: e.tensor_copy(out=thr[:], in_=psB[:, 4:6]),
                  reads=["psB0"], writes=["thr"])
                E("V", lambda e: e.tensor_scalar(out=nthr[:], in0=thr[:], scalar1=-1.0,
                                                 scalar2=None, op0=OP.mult),
                  reads=["thr"], writes=["nthr"])
                if tap("thrtap", [128, 2]) is not None:
                    dma("QG", dbg_out["thrtap"], thr[:], ["thr"], ["tapthr"])
                if tap("gamma", [1, 2]) is not None:
                    dma("QG", dbg_out["gamma"], gam[:], ["gam"], ["tapgam"])

            if stop_after == "P0":
                emit_gamma()
                emit_gamma_thr()
                _finish(rec, E, dma)
                _replay(nc, rec)
                return nc, dbg_out

            with ExitStack() as p1ctx:
                winb = [sbuf(p1ctx, f"winb{k}", [128, 2 * D], BF16) for k in range(DB)]
                xin2 = [[sbuf(p1ctx, f"xin{p}_{k}", [128, TC + HL], BF16)
                         for k in range(DB)] for p in range(2)]
                xlnt2 = [[sbuf(p1ctx, f"xlnt{p}_{k}", [128, TC], BF16)
                          for k in range(DB)] for p in range(2)]
                for k in range(DB):
                    i = si[0] % 2; si[0] += 1
                    dma(SQ[i], stg(i)[:, 0:2 * D], w_in_t[k * 128:(k + 1) * 128, :],
                        ["EXT"], [f"stg{i}"])
                    E("S", lambda e, k=k, i=i: e.copy(out=winb[k][:], in_=stg(i)[:, 0:2 * D]),
                      reads=[f"stg{i}"], writes=[f"winb{k}"])

                def p1_stats(col0, w):
                    tiles = [((lambda k=k, col0=col0, w=w: xsb[k][:, col0:col0 + w]), f"x{k}")
                             for k in range(DB)]
                    return emit_ln(tiles, w)

                def p1_norm(par, col0, w, kr, kmr):
                    xl = xlnt2[par]
                    for k in range(DB):
                        E("V", lambda e, k=k, col0=col0, w=w, xl=xl: e.tensor_tensor(
                            out=xl[k][:, 0:w], in0=xsb[k][:, col0:col0 + w],
                            in1=rstbc[:, 0:w], op=OP.mult),
                          reads=[f"x{k}", kr], writes=[f"xln{par}_{k}"])
                        E("V", lambda e, k=k, w=w, xl=xl: e.tensor_tensor(
                            out=xl[k][:, 0:w], in0=xl[k][:, 0:w],
                            in1=rstbc[:, w:2 * w], op=OP.subtract),
                          reads=[f"xln{par}_{k}", kmr], writes=[f"xln{par}_{k}"])

                def p1_bulk(cid, par, col0, w, full):
                    xl, xin = xlnt2[par], xin2[par]
                    ots = list(range(16)) if full else list(range(8))
                    for ot in ots:
                        pcol = (ot % 4) * 512
                        for k in range(DB):
                            E("T", lambda e, ot=ot, k=k, pcol=pcol, w=w, xl=xl: e.matmul(
                                out=psA[:, pcol:pcol + w],
                                lhsT=winb[k][:, ot * 128:(ot + 1) * 128],
                                rhs=xl[k][:, 0:w],
                                start=(k == 0), stop=(k == DB - 1)),
                              reads=[f"xln{par}_{k}", f"winb{k}"], writes=[f"psA{ot % 4}"],
                              inc=(k == DB - 1))
                        if ot < 8:
                            dst0 = HL if full else 0
                            E("S", lambda e, ot=ot, pcol=pcol, w=w, dst0=dst0, xin=xin: e.copy(
                                out=xin[ot][:, dst0:dst0 + w], in_=psA[:, pcol:pcol + w]),
                              reads=[f"psA{ot % 4}"], writes=[f"xinp{par}_{ot}"])
                        else:
                            i = ot % 2
                            E("S", lambda e, pcol=pcol, w=w, i=i: e.activation(
                                out=szr[i][:, 0:w], in_=psA[:, pcol:pcol + w], func=AF.Silu),
                              reads=[f"psA{ot % 4}"], writes=[f"szr{i}"])
                            dma("QA", sz_dram[(ot - 8) * 128:(ot - 7) * 128,
                                              col0 - HL:col0 - HL + w],
                                szr[i][:, 0:w], [f"szr{i}"], [f"szd{ot - 8}"])
                    if not full:
                        return
                    tok0 = col0 - HL
                    for k in range(DB):
                        E("V", lambda e, k=k, w=w, xin=xin: e.tensor_scalar(
                            out=cv[0][:, 0:w], in0=xin[k][:, 0:w],
                            scalar1=cw[:, 4 * k:4 * k + 1], scalar2=None, op0=OP.mult),
                          reads=[f"xinp{par}_{k}", "cw"], writes=["cv0"])
                        E("V", lambda e, k=k, w=w, xin=xin: e.tensor_scalar(
                            out=cv[1][:, 0:w], in0=xin[k][:, 1:1 + w],
                            scalar1=cw[:, 4 * k + 1:4 * k + 2], scalar2=None, op0=OP.mult),
                          reads=[f"xinp{par}_{k}", "cw"], writes=["cv1"])
                        E("V", lambda e, w=w: e.tensor_tensor(
                            out=cv[0][:, 0:w], in0=cv[0][:, 0:w], in1=cv[1][:, 0:w], op=OP.add),
                          reads=["cv0", "cv1"], writes=["cv0"])
                        E("V", lambda e, k=k, w=w, xin=xin: e.tensor_scalar(
                            out=cv[1][:, 0:w], in0=xin[k][:, 2:2 + w],
                            scalar1=cw[:, 4 * k + 2:4 * k + 3], scalar2=None, op0=OP.mult),
                          reads=[f"xinp{par}_{k}", "cw"], writes=["cv1"])
                        E("V", lambda e, k=k, w=w, xin=xin: e.tensor_scalar(
                            out=cv[2][:, 0:w], in0=xin[k][:, 3:3 + w],
                            scalar1=cw[:, 4 * k + 3:4 * k + 4], scalar2=None, op0=OP.mult),
                          reads=[f"xinp{par}_{k}", "cw"], writes=["cv2"])
                        E("V", lambda e, w=w: e.tensor_tensor(
                            out=cv[1][:, 0:w], in0=cv[1][:, 0:w], in1=cv[2][:, 0:w], op=OP.add),
                          reads=["cv1", "cv2"], writes=["cv1"])
                        E("V", lambda e, w=w: e.tensor_tensor(
                            out=cv[0][:, 0:w], in0=cv[0][:, 0:w], in1=cv[1][:, 0:w], op=OP.add),
                          reads=["cv0", "cv1"], writes=["cv0"])
                        E("S", lambda e, k=k, tok0=tok0, w=w: e.activation(
                            out=xcar[k][:, tok0:tok0 + w], in_=cv[0][:, 0:w], func=AF.Silu),
                          reads=["cv0"], writes=[f"xc{k}"])
                        if cid + 1 < NTC:
                            E("V", lambda e, k=k, w=w, xin=xin: e.tensor_copy(
                                out=xin2[par ^ 1][k][:, 0:HL], in_=xin[k][:, w:w + HL]),
                              reads=[f"xinp{par}_{k}"], writes=[f"xinp{par ^ 1}_{k}"])

                def p1_wx(cid, col0, w):
                    tok0 = col0 - HL
                    for k in range(DB):
                        E("T", lambda e, k=k, tok0=tok0, w=w: e.matmul(
                            out=psB[0:N, 0:w], lhsT=wxdt[k][:],
                            rhs=xcar[k][:, tok0:tok0 + w],
                            start=(k == 0), stop=(k == DB - 1)),
                          reads=[f"xc{k}", f"wxdt{k}"], writes=["psB0"], inc=(k == DB - 1))
                    E("S", lambda e, tok0=tok0, w=w: e.copy(
                        out=dtlow[:, tok0:tok0 + w], in_=psB[0:N, 0:w]),
                      reads=["psB0"], writes=["dtlow"])
                    for k in range(DB):
                        E("T", lambda e, k=k, tok0=tok0, w=w: e.matmul(
                            out=psB[0:N, 512:512 + w], lhsT=wxB[k][:],
                            rhs=xcar[k][:, tok0:tok0 + w],
                            start=(k == 0), stop=(k == DB - 1)),
                          reads=[f"xc{k}", f"wxB{k}"], writes=["psB1"], inc=(k == DB - 1))
                    i = cid % 2
                    E("S", lambda e, i=i, w=w: e.copy(out=bmr[i][:, 0:w],
                                                      in_=psB[0:N, 512:512 + w]),
                      reads=["psB1"], writes=[f"bmr{i}"])
                    dma("QA", bmat_dram[:, tok0:tok0 + w], bmr[i][:, 0:w],
                        [f"bmr{i}"], ["bmatd"])

                if p1_nchunks >= 0:
                    rec.scope("P1")
                kr, kmr = p1_stats(0, HL)
                p1_norm(0, 0, HL, kr, kmr)
                p1_bulk(-1, 0, 0, HL, full=False)
                nch = max(p1_nchunks, 0)
                if nch:
                    kr, kmr = p1_stats(HL, TC)
                    p1_norm(0, HL, TC, kr, kmr)
                for tc in range(nch):
                    if tc + 1 < nch:
                        krn, kmrn = p1_stats(HL + (tc + 1) * TC, TC)
                    p1_bulk(tc, tc % 2, HL + tc * TC, TC, full=True)
                    if tc == 1:
                        emit_gamma()
                    if tc > 0:
                        p1_wx(tc - 1, HL + (tc - 1) * TC, TC)
                    if tc + 1 < nch:
                        p1_norm((tc + 1) % 2, HL + (tc + 1) * TC, TC, krn, kmrn)
                if nch:
                    p1_wx(nch - 1, HL + (nch - 1) * TC, TC)
                if nch < 2:
                    emit_gamma()

                if tap("xc", [D, T]) is not None:
                    for k in range(DB):
                        dma("QG", dbg_out["xc"][k * 128:(k + 1) * 128, :], xcar[k][:],
                            [f"xc{k}"], ["tapxc"])
                if tap("dtlow", [N, T]) is not None:
                    dma("QG", dbg_out["dtlow"].bitcast(BF16)[:, 0:T], dtlow[:],
                        ["dtlow"], ["tapdtl"]) if False else \
                        dma("QG", dbg_out["dtlow"], dtlow[:], ["dtlow"], ["tapdtl"])

            if stop_after == "P1":
                _finish(rec, E, dma)
                _replay(nc, rec)
                return nc, dbg_out

            rec.scope("P2")
            rec.full_barrier()
            with ExitStack() as p2ctx:
                dt_t = sbuf(p2ctx, "dt_t", [128, T], F32)
                sdt_t = sbuf(p2ctx, "sdt_t", [128, T], F32)
                softr = sdt_t
                softt = sbuf(p2ctx, "softt", [128, T], F32)
                a_t = [sbuf(p2ctx, f"a_t{i}", [128, T], F32) for i in range(3)]
                bc_t = [sbuf(p2ctx, f"bc_t{i}", [128, T], BF16) for i in range(2)]
                b_t = [sbuf(p2ctx, f"b_t{i}", [128, T], BF16) for i in range(2)]
                h_t = [sbuf(p2ctx, f"h_t{i}", [128, T], BF16) for i in range(3)]
                yst = [sbuf(p2ctx, f"yst{i}", [128, TC], BF16) for i in range(2)]

                def emit_dt(k):
                    for tc4 in range(NTC):
                        E("T", lambda e, k=k, tc4=tc4: e.matmul(
                            out=psB[:, tc4 * TC:(tc4 + 1) * TC], lhsT=wdtb[k][:],
                            rhs=dtlow[:, tc4 * TC:(tc4 + 1) * TC], start=True, stop=True),
                          reads=["dtlow", f"wdtb{k}"], writes=[f"psB{tc4}"])
                    E("S", lambda e, k=k: e.activation(out=softr[:], in_=psB[:, 0:T],
                                                       func=AF.Relu, bias=bdt[:, k:k + 1]),
                      reads=["psB0", "psB1", "psB2", "psB3", "bdt"],
                      writes=["softr", "sdt"])
                    E("S", lambda e, k=k: e.activation(out=softt[:], in_=psB[:, 0:T],
                                                       func=AF.Abs, bias=bdt[:, k:k + 1]),
                      reads=["psB0", "psB1", "psB2", "psB3", "bdt"], writes=["softt"],
                      fence=True)
                    E("S", lambda e: e.activation(out=softt[:], in_=softt[:], func=AF.Exp,
                                                  scale=-1.0),
                      reads=["softt"], writes=["softt"], fence=True)
                    E("S", lambda e: e.activation(out=softt[:], in_=softt[:], func=AF.Ln,
                                                  bias=1.0),
                      reads=["softt"], writes=["softt"], fence=True)
                    E("V", lambda e: e.tensor_tensor(out=dt_t[:], in0=softr[:], in1=softt[:],
                                                     op=OP.add),
                      reads=["softr", "softt"], writes=["dt"])
                    E("V", lambda e: e.tensor_tensor_scan(out=sdt_t[:], data0=dt_t[:],
                                                          data1=dt_t[:], initial=0.0,
                                                          op0=OP.add, op1=OP.max),
                      reads=["dt", "softr"], writes=["sdt", "softr"], fence=True)

                for k in range(DB):
                    emit_dt(k)
                    E("V", lambda e, k=k: e.tensor_scalar(
                        out=ap16[:], in0=A_sb[k][:], scalar1=sdt_t[:, T - 1:T],
                        scalar2=None, op0=OP.mult),
                      reads=[f"A{k}", "sdt"], writes=["ap16"])
                    E("S", lambda e, k=k: e.activation(out=aprod[:, k * N:(k + 1) * N],
                                                       in_=ap16[:], func=AF.Exp),
                      reads=["ap16"], writes=["aprod"])
                    for n in range(N):
                        i = n % 3
                        ib = n % 2
                        ih = n % 3
                        ln = k * N + n
                        E("S", lambda e, k=k, n=n, i=i: e.activation(
                            out=a_t[i][:], in_=dt_t[:], func=AF.Exp,
                            scale=A_sb[k][:, n:n + 1]),
                          reads=["dt", f"A{k}"], writes=[f"a{i}"])
                        dma("QS", bc_t[ib][:],
                            bmat_dram[n:n + 1, :].partition_broadcast(128).squeeze(1),
                            ["bmatd"], [f"bcl{ib}"])
                        E("V", lambda e, k=k, ib=ib: e.tensor_tensor(
                            out=b_t[ib][:], in0=bc_t[ib][:], in1=xcar[k][:], op=OP.mult),
                          reads=[f"bcl{ib}", f"xc{k}"], writes=[f"b{ib}"])
                        E("V", lambda e, i=i, ib=ib: e.tensor_tensor_scan(
                            out=h_t[i][:], data0=a_t[i][:], data1=b_t[ib][:], initial=0.0,
                            op0=OP.mult, op1=OP.add),
                          reads=[f"a{i}", f"b{ib}"], writes=[f"h{i}"], fence=True)
                        E("V", lambda e, i=i, ln=ln: e.tensor_copy(
                            out=hend[:, ln:ln + 1], in_=h_t[i][:, T - 1:T]),
                          reads=[f"h{i}"], writes=["hend"])
                        for tc4 in range(NTC):
                            E("T", lambda e, i=i, tc4=tc4, n=n: e.matmul(
                                out=psA[:, tc4 * TC:(tc4 + 1) * TC], lhsT=eye_bf[:],
                                rhs=h_t[i][:, tc4 * TC:(tc4 + 1) * TC],
                                start=(n == 0), stop=(n == N - 1)),
                              reads=[f"h{i}", "eye"], writes=[f"psA{tc4}"],
                              inc=(tc4 == NTC - 1))
                    for tc4 in range(NTC):
                        j = tc4 % 2
                        E("S", lambda e, tc4=tc4, j=j: e.copy(
                            out=yst[j][:], in_=psA[:, tc4 * TC:(tc4 + 1) * TC]),
                          reads=[f"psA{tc4}"], writes=[f"yst{j}"])
                        dma("QA", yl_dram[k * 128:(k + 1) * 128, tc4 * TC:(tc4 + 1) * TC],
                            yst[j][:], [f"yst{j}"], ["yld"])

                if tap("yloc", [D, T], BF16) is not None:
                    rec.drain("QG", "p")
                    dma("QG", dbg_out["yloc"], yl_dram, ["yld"], ["tapyl"])
                if tap("hend", [128, 128]) is not None:
                    dma("QG", dbg_out["hend"], hend[:], ["hend"], ["taphe"])
                if tap("aprod", [128, 128]) is not None:
                    dma("QG", dbg_out["aprod"], aprod[:], ["aprod"], ["tapap"])

                if stop_after == "P2":
                    _finish(rec, E, dma)
                    _replay(nc, rec)
                    return nc, dbg_out

                rec.scope("P3")
                dma("QA", cc_send_t.ap()[0:128, :], hend[:], ["hend"], ["ccs"])
                dma("QA", cc_send_t.ap()[128:256, :], aprod[:], ["aprod"], ["ccs"])
                E("CC", lambda e: e.collective_compute(
                    "AllGather", OP.bypass, replica_groups=[list(range(NCORES))],
                    ins=[cc_send_t.ap().opt()], outs=[cc_recv_t.ap().opt()]),
                  reads=["ccs"], writes=["ccr"])
                E("V", lambda e: e.memset(h0[:], 0.0), writes=["h0"])
                for j in range(NCORES):
                    dma("QS", hrec[0][:], cc_recv_t.ap()[j * 256:j * 256 + 128, :],
                        ["ccr"], ["hr0"])
                    dma("QS", hrec[1][:], cc_recv_t.ap()[j * 256 + 128:j * 256 + 256, :],
                        ["ccr"], ["hr1"])
                    E("V", lambda e, j=j: e.tensor_scalar(
                        out=hP[:], in0=hrec[0][:], scalar1=msk[:, j:j + 1], scalar2=None,
                        op0=OP.mult),
                      reads=["hr0", "msk"], writes=["hP"])
                    E("V", lambda e, j=j: e.tensor_scalar(
                        out=hrec[1][:], in0=hrec[1][:], scalar1=1.0, scalar2=msk[:, j:j + 1],
                        op0=OP.subtract, op1=OP.mult),
                      reads=["hr1", "msk"], writes=["hr1"])
                    E("V", lambda e: e.tensor_scalar(
                        out=hrec[1][:], in0=hrec[1][:], scalar1=1.0, scalar2=None, op0=OP.add),
                      reads=["hr1"], writes=["hr1"])
                    E("V", lambda e: e.tensor_tensor(out=h0[:], in0=h0[:], in1=hrec[1][:],
                                                     op=OP.mult),
                      reads=["h0", "hr1"], writes=["h0"])
                    E("V", lambda e: e.tensor_tensor(out=h0[:], in0=h0[:], in1=hP[:],
                                                     op=OP.add),
                      reads=["h0", "hP"], writes=["h0"])
                if tap("h0", [128, 128]) is not None:
                    dma("QG", dbg_out["h0"], h0[:], ["h0"], ["taph0"])

            rec.scope("P4a")
            rec.full_barrier()
            with ExitStack() as p4ctx:
                dt_t = sbuf(p4ctx, "dt4", [128, T], F32)
                sdt_t = sbuf(p4ctx, "sdt4", [128, T], F32)
                softr = sdt_t
                softt = sbuf(p4ctx, "softt4", [128, T], F32)
                cuma = [sbuf(p4ctx, f"cuma{i}", [128, T], BF16) for i in range(3)]
                diag = [sbuf(p4ctx, f"diag{i}", [128, 128], BF16) for i in range(3)]
                szk = [sbuf(p4ctx, f"szk{i}", [128, T], BF16) for i in range(2)]
                ylk = [sbuf(p4ctx, "ylk0", [128, T], BF16)]
                woutb = [sbuf(p4ctx, f"woutb{k}", [128, D], BF16) for k in range(DB)]
                x1st = [sbuf(p4ctx, f"x1st{i}", [128, TC], BF16) for i in range(2)]

                for k in range(DB):
                    i = si[0] % 2; si[0] += 1
                    dma("QS", stg(i)[:, 0:D], w_out_t[k * 128:(k + 1) * 128, :],
                        ["EXT"], [f"stg{i}"])
                    E("S", lambda e, k=k, i=i: e.copy(out=woutb[k][:], in_=stg(i)[:, 0:D]),
                      reads=[f"stg{i}"], writes=[f"woutb{k}"])

                def emit_dt4(k):
                    for tc4 in range(NTC):
                        E("T", lambda e, k=k, tc4=tc4: e.matmul(
                            out=psB[:, tc4 * TC:(tc4 + 1) * TC], lhsT=wdtb[k][:],
                            rhs=dtlow[:, tc4 * TC:(tc4 + 1) * TC], start=True, stop=True),
                          reads=["dtlow", f"wdtb{k}"], writes=[f"psB{tc4}"])
                    E("S", lambda e, k=k: e.activation(out=softr[:], in_=psB[:, 0:T],
                                                       func=AF.Relu, bias=bdt[:, k:k + 1]),
                      reads=["psB0", "psB1", "psB2", "psB3", "bdt"],
                      writes=["softr", "sdt"])
                    E("S", lambda e, k=k: e.activation(out=softt[:], in_=psB[:, 0:T],
                                                       func=AF.Abs, bias=bdt[:, k:k + 1]),
                      reads=["psB0", "psB1", "psB2", "psB3", "bdt"], writes=["softt"],
                      fence=True)
                    E("S", lambda e: e.activation(out=softt[:], in_=softt[:], func=AF.Exp,
                                                  scale=-1.0),
                      reads=["softt"], writes=["softt"], fence=True)
                    E("S", lambda e: e.activation(out=softt[:], in_=softt[:], func=AF.Ln,
                                                  bias=1.0),
                      reads=["softt"], writes=["softt"], fence=True)
                    E("V", lambda e: e.tensor_tensor(out=dt_t[:], in0=softr[:], in1=softt[:],
                                                     op=OP.add),
                      reads=["softr", "softt"], writes=["dt"])
                    E("V", lambda e: e.tensor_tensor_scan(out=sdt_t[:], data0=dt_t[:],
                                                          data1=dt_t[:], initial=0.0,
                                                          op0=OP.add, op1=OP.max),
                      reads=["dt", "softr"], writes=["sdt", "softr"], fence=True)

                for k in range(DB):
                    emit_dt4(k)
                    i = k % 2
                    dma("QA", szk[i][:], sz_dram[k * 128:(k + 1) * 128, :],
                        [f"szd{k}"], [f"szk{i}"])
                    dma("QA", ylk[0][:], yl_dram[k * 128:(k + 1) * 128, :],
                        ["yld"], ["ylk0"])
                    for tc4 in range(NTC):
                        E("T", lambda e, tc4=tc4: e.matmul(
                            out=psA[:, tc4 * TC:(tc4 + 1) * TC], lhsT=eye_bf[:],
                            rhs=ylk[0][:, tc4 * TC:(tc4 + 1) * TC],
                            start=True, stop=False),
                          reads=["ylk0", "eye"], writes=[f"psA{tc4}"], inc=False)
                    for n in range(N):
                        j = n % 3
                        ln = k * N + n
                        E("S", lambda e, k=k, n=n, j=j: e.activation(
                            out=cuma[j][:], in_=sdt_t[:], func=AF.Exp,
                            scale=A_sb[k][:, n:n + 1]),
                          reads=["sdt", f"A{k}"], writes=[f"cu{j}"])
                        E("V", lambda e, j=j, ln=ln: e.tensor_scalar(
                            out=diag[j][:], in0=eye_bf[:], scalar1=h0[:, ln:ln + 1],
                            scalar2=None, op0=OP.mult),
                          reads=["eye", "h0"], writes=[f"dg{j}"])
                        for tc4 in range(NTC):
                            E("T", lambda e, j=j, tc4=tc4, n=n: e.matmul(
                                out=psA[:, tc4 * TC:(tc4 + 1) * TC], lhsT=diag[j][:],
                                rhs=cuma[j][:, tc4 * TC:(tc4 + 1) * TC],
                                start=False, stop=(n == N - 1)),
                              reads=[f"cu{j}", f"dg{j}"], writes=[f"psA{tc4}"],
                              inc=(tc4 == NTC - 1))
                    for tc4 in range(NTC):
                        E("V", lambda e, k=k, i=i, tc4=tc4: e.tensor_tensor(
                            out=xcar[k][:, tc4 * TC:(tc4 + 1) * TC],
                            in0=psA[:, tc4 * TC:(tc4 + 1) * TC],
                            in1=szk[i][:, tc4 * TC:(tc4 + 1) * TC], op=OP.mult),
                          reads=[f"psA{tc4}", f"szk{i}"], writes=[f"g{k}", f"xc{k}"])
                if tap("gated", [D, T]) is not None:
                    for k in range(DB):
                        dma("QG", dbg_out["gated"][k * 128:(k + 1) * 128, :], xcar[k][:],
                            [f"g{k}"], ["tapgt"])

                rec.scope("P4a2")
                for tc in range(NTC):
                    for ot in range(DB):
                        pcol = (ot % 2) * 512
                        for k in range(DB):
                            E("T", lambda e, ot=ot, k=k, pcol=pcol, tc=tc: e.matmul(
                                out=psA[:, pcol:pcol + TC],
                                lhsT=woutb[k][:, ot * 128:(ot + 1) * 128],
                                rhs=xcar[k][:, tc * TC:(tc + 1) * TC],
                                start=(k == 0), stop=(k == DB - 1)),
                              reads=[f"g{k}", f"woutb{k}"], writes=[f"psA{ot % 2}"],
                              inc=(k == DB - 1))
                        E("S", lambda e, ot=ot, pcol=pcol: e.copy(
                            out=mo_t[ot][:], in_=psA[:, pcol:pcol + TC]),
                          reads=[f"psA{ot % 2}"], writes=[f"mo{ot}"])
                    tiles = [((lambda k=k: mo_t[k][:]), f"mo{k}") for k in range(DB)]
                    kr, kmr = emit_ln(tiles, TC)
                    for k in range(DB):
                        j = k % 2
                        E("V", lambda e, k=k, j=j: e.tensor_tensor(
                            out=tmp_bf[j][:], in0=mo_t[k][:], in1=rstbc[:, 0:TC], op=OP.mult),
                          reads=[f"mo{k}", kr], writes=[f"tb{j}"])
                        E("V", lambda e, j=j: e.tensor_tensor(
                            out=tmp_bf[j][:], in0=tmp_bf[j][:], in1=rstbc[:, TC:2 * TC],
                            op=OP.subtract),
                          reads=[f"tb{j}", kmr], writes=[f"tb{j}"])
                        c0 = HL + tc * TC
                        E("V", lambda e, k=k, j=j, c0=c0: e.tensor_tensor(
                            out=x1st[j][:], in0=xsb[k][:, c0:c0 + TC], in1=tmp_bf[j][:],
                            op=OP.add),
                          reads=[f"tb{j}", f"x{k}"], writes=[f"x1st{j}"])
                        dma("QA", x1_dram[k * 128:(k + 1) * 128, tc * TC:(tc + 1) * TC],
                            x1st[j][:], [f"x1st{j}"], ["x1d"])
                if tap("x1", [D, T], BF16) is not None:
                    rec.drain("QG", "p")
                    dma("QG", dbg_out["x1"], x1_dram, ["x1d"], ["tapx1"])

            if stop_after == "P4a":
                _finish(rec, E, dma)
                _replay(nc, rec)
                return nc, dbg_out

        rec.scope("P4b")
        rec.full_barrier()
        emit_gamma_thr()
        with ExitStack() as p5ctx:
            wfcq = [sbuf(p5ctx, f"wfcq{k}", [128, HD], BF16) for k in range(DB)]
            wprq = [sbuf(p5ctx, f"wprq{j}", [128, D], BF16) for j in range(JT)]
            NGW = 6
            gwst = [sbuf(p5ctx, f"gwst{i}", [128, 256], F32) for i in range(NGW)]
            qge = sbuf(p5ctx, "qge", [128, 256], BF16)
            qle = sbuf(p5ctx, "qle", [128, 256], BF16)
            x1k = [sbuf(p5ctx, f"x1k{i}", [128, TF], BF16) for i in range(2 * DB)]
            xl2 = [sbuf(p5ctx, f"xl2{k}", [128, TF], BF16) for k in range(2 * DB)]
            gtl = [sbuf(p5ctx, f"gtl{i}", [128, TF], BF16) for i in range(JT)]
            fo_t = mo_t

            def quant_piece(i, dst, dkey, thrcol, nthrcol):
                E("V", lambda e: e.tensor_scalar(
                    out=qge[:], in0=gwst[i][:], scalar1=thrcol, scalar2=None,
                    op0=OP.is_ge),
                  reads=[f"gq{i}", "thr"], writes=["qge"])
                E("V", lambda e: e.tensor_scalar(
                    out=qle[:], in0=gwst[i][:], scalar1=nthrcol, scalar2=None,
                    op0=OP.is_le),
                  reads=[f"gq{i}", "nthr"], writes=["qle"])
                E("V", lambda e: e.tensor_tensor(
                    out=dst, in0=qge[:], in1=qle[:], op=OP.subtract),
                  reads=["qge", "qle"], writes=[dkey])
            pcnt = [0]
            QROT = ("QS", "QA", "QG")
            for k in range(DB):
                for piece in range(16):
                    i = pcnt[0] % NGW
                    q = QROT[pcnt[0] % 3]
                    pcnt[0] += 1
                    dma(q, gwst[i][:],
                        w_fc_t[k * 128:(k + 1) * 128, piece * 256:(piece + 1) * 256],
                        ["EXT"], [f"gq{i}"])
                    quant_piece(i, wfcq[k][:, piece * 256:(piece + 1) * 256],
                                f"wfcq{k}", thr[:, 0:1], nthr[:, 0:1])
            for j in range(JT):
                for piece in range(4):
                    i = pcnt[0] % NGW
                    q = QROT[pcnt[0] % 3]
                    pcnt[0] += 1
                    dma(q, gwst[i][:],
                        w_proj_t[j * 128:(j + 1) * 128, piece * 256:(piece + 1) * 256],
                        ["EXT"], [f"gq{i}"])
                    quant_piece(i, wprq[j][:, piece * 256:(piece + 1) * 256],
                                f"wprq{j}", thr[:, 1:2], nthr[:, 1:2])
                for ot in range(DB):
                    dst = (psA[:, ot * 512:ot * 512 + 1] if ot < 4
                           else psB[:, (ot - 4) * 512:(ot - 4) * 512 + 1])
                    bkey = (f"psA{ot}" if ot < 4 else f"psB{ot - 4}")
                    E("T", lambda e, j=j, ot=ot, dst=dst: e.matmul(
                        out=dst, lhsT=wprq[j][:, ot * 128:(ot + 1) * 128], rhs=lnbc[j][:],
                        start=(j == 0), stop=(j == JT - 1)),
                      reads=[f"wprq{j}", f"lnb{j}"], writes=[bkey],
                      inc=(ot == DB - 1))
                E("V", lambda e, j=j: e.tensor_scalar(
                    out=wprq[j][:], in0=wprq[j][:], scalar1=lnwf_sb[:, j:j + 1],
                    scalar2=None, op0=OP.mult),
                  reads=[f"wprq{j}", "lnwf"], writes=[f"wprq{j}"])
            E("V", lambda e: e.tensor_copy(out=wlnb_f[:, 0:4], in_=psA[:, 0:1537:512]),
              reads=["psA0", "psA1", "psA2", "psA3"], writes=["wlnb"])
            E("V", lambda e: e.tensor_copy(out=wlnb_f[:, 4:8], in_=psB[:, 0:1537:512]),
              reads=["psB0", "psB1", "psB2", "psB3"], writes=["wlnb"])
            for j in range(JT):
                E("T", lambda e, j=j: e.matmul(
                    out=psB[0:1, 0:512], lhsT=onescb[:],
                    rhs=wprq[j][:, 0:512], start=(j == 0), stop=(j == JT - 1)),
                  reads=[f"wprq{j}", "onescb", "wlnb"], writes=["psB0"], inc=False)
                E("T", lambda e, j=j: e.matmul(
                    out=psB[0:1, 512:1024], lhsT=onescb[:],
                    rhs=wprq[j][:, 512:1024], start=(j == 0), stop=(j == JT - 1)),
                  reads=[f"wprq{j}", "onescb", "wlnb"], writes=["psB1"],
                  inc=(j == JT - 1))
            E("V", lambda e: e.tensor_copy(out=wsum_bf[:], in_=psB[0:1, 0:D]),
              reads=["psB0", "psB1"], writes=["wsum"])
            for nm in ("psB0", "psB1", "psB2", "psB3"):
                pass

            if tap("wlnb", [128, DB]) is not None:
                dma("QG", dbg_out["wlnb"], wlnb_f[:], ["wlnb"], ["tapwlnb"])
            if tap("wqfc", [128, HD], BF16) is not None:
                dma("QG", dbg_out["wqfc"], wfcq[0][:], ["wfcq0"], ["tapwqfc"])
            if tap("wqpr", [128, D], BF16) is not None:
                dma("QG", dbg_out["wqpr"], wprq[0][:], ["wprq0"], ["tapwqpr"])
            if tap("wsum", [1, D]) is not None:
                rec.drain("V", "p")
                E("QG", lambda e: e.dma_start(out=dbg_out["wsum"],
                                              in_=wsum_bf[:].bitcast(BF16)),
                  reads=["wsum"], writes=["tapws"]) if False else \
                    dma("QG", dbg_out["wsum"].bitcast(BF16)[:, 0:D], wsum_bf[:],
                        ["wsum"], ["tapws"])

            tap("ffo", [D, T], BF16)
            rec.scope("P5")

            def p5_A(tf):
                s = (tf % 2) * DB
                t0 = tf * TF
                for k in range(DB):
                    dma("QA", x1k[s + k][:], x1_dram[k * 128:(k + 1) * 128, t0:t0 + TF],
                        ["x1d"], [f"x1k{s + k}"])
                tiles = [((lambda k=k, s=s: x1k[s + k][:]), f"x1k{s + k}") for k in range(DB)]
                kr, kmr = emit_ln(tiles, TF)
                for k in range(DB):
                    E("V", lambda e, k=k, s=s: e.tensor_tensor(
                        out=xl2[s + k][:], in0=x1k[s + k][:], in1=rstbc[:, 0:TF], op=OP.mult),
                      reads=[f"x1k{s + k}", kr], writes=[f"xl2_{s + k}"])
                    E("V", lambda e, k=k, s=s: e.tensor_tensor(
                        out=xl2[s + k][:], in0=xl2[s + k][:], in1=rstbc[:, TF:2 * TF],
                        op=OP.subtract),
                      reads=[f"xl2_{s + k}", kmr], writes=[f"xl2_{s + k}"])

            def p5_B(tf):
                s = (tf % 2) * DB

                for j in range(JT):
                    pcol = (j % 2) * 512
                    for k in range(DB):
                        E("T", lambda e, j=j, k=k, pcol=pcol, s=s: e.matmul(
                            out=psB[:, pcol:pcol + TF],
                            lhsT=wfcq[k][:, j * 128:(j + 1) * 128], rhs=xl2[s + k][:],
                            start=(k == 0), stop=(k == DB - 1)),
                          reads=[f"xl2_{s + k}", f"wfcq{k}"], writes=[f"psB{j % 2}"],
                          inc=(k == DB - 1))
                    gt = gtl[j]
                    E("S", lambda e, j=j, pcol=pcol, gt=gt: e.activation(
                        out=gt[:], in_=psB[:, pcol:pcol + TF], func=AF.Gelu),
                      reads=[f"psB{j % 2}"], writes=[f"gt{j}"])
                for j in range(JT):
                    E("V", lambda e, j=j: e.tensor_tensor(
                        out=g2r[j % 4][:], in0=gtl[j][:], in1=gtl[j][:], op=OP.mult),
                      reads=[f"gt{j}"], writes=[f"g2_{j % 4}"])
                    E("T", lambda e, j=j: e.matmul(
                        out=psA[0:1, 0:TF], lhsT=onescb[:], rhs=g2r[j % 4][:],
                        start=(j == 0), stop=(j == JT - 1)),
                      reads=[f"g2_{j % 4}", "onescb"], writes=["psA0"], inc=True)

            def p5_C(tf):
                for j in range(JT):
                    E("T", lambda e, j=j: e.matmul(
                        out=psB[0:1, 1536:1536 + TF], lhsT=onescb[:], rhs=gtl[j][:],
                        start=(j == 0), stop=(j == JT - 1)),
                      reads=[f"gt{j}", "onescb"], writes=["psB3"],
                      inc=(j == JT - 1))
                E("V", lambda e: e.tensor_scalar(out=srf[:, 0:TF],
                                                 in0=psA[0:1, 0:TF],
                                                 scalar1=1.0 / HD, scalar2=None, op0=OP.mult),
                  reads=["psA0"], writes=["sv"])
                E("V", lambda e: e.tensor_scalar(out=srf[:, TF:2 * TF],
                                                 in0=psB[0:1, 1536:1536 + TF],
                                                 scalar1=1.0 / HD, scalar2=None, op0=OP.mult),
                  reads=["psB3"], writes=["sm"])
                E("V", lambda e: e.tensor_tensor(out=srf[:, 2 * TF:3 * TF],
                                                 in0=srf[:, TF:2 * TF],
                                                 in1=srf[:, TF:2 * TF], op=OP.mult),
                  reads=["sm"], writes=["smr"])
                E("V", lambda e: e.tensor_tensor(out=srf[:, 0:TF], in0=srf[:, 0:TF],
                                                 in1=srf[:, 2 * TF:3 * TF], op=OP.subtract),
                  reads=["smr", "sv"], writes=["sv"])
                E("S", lambda e: e.activation(out=srf[:, 0:TF], in_=srf[:, 0:TF],
                                              func=AF.Ln, bias=cns[0:1, 0:1]),
                  reads=["sv", "cns"], writes=["sv"])
                E("S", lambda e: e.activation(out=srb[:, 0:TF], in_=srf[:, 0:TF],
                                              func=AF.Exp, scale=cns[0:1, 1:2]),
                  reads=["sv", "cns"], writes=["srbr"])
                E("V", lambda e: e.tensor_scalar(out=srf[:, 2 * TF:3 * TF],
                                                 in0=srf[:, TF:2 * TF], scalar1=-1.0,
                                                 scalar2=None, op0=OP.mult),
                  reads=["sm"], writes=["smr"])
                E("V", lambda e: e.tensor_copy(out=negm_bf[:], in_=srf[:, 2 * TF:3 * TF]),
                  reads=["smr"], writes=["negm"])
                E("T", lambda e: e.matmul(out=psB[:, 1024:1024 + TF], lhsT=onesrb[:],
                                          rhs=srb[:, 0:TF], start=True, stop=True),
                  reads=["srbr", "onesrb"], writes=["psB2"])
                E("S", lambda e: e.copy(out=rstbc[:, 0:TF], in_=psB[:, 1024:1024 + TF]),
                  reads=["psB2"], writes=["rsthl"])

            def p5_D(tf, half):
                for j in range(JT):
                    for oth in range(4):
                        ot = half * 4 + oth
                        dst = psA[:, oth * 512:oth * 512 + TF]
                        E("T", lambda e, j=j, ot=ot, dst=dst: e.matmul(
                            out=dst, lhsT=wprq[j][:, ot * 128:(ot + 1) * 128],
                            rhs=gtl[j][:], start=(j == 0), stop=False),
                          reads=[f"gt{j}", f"wprq{j}"], writes=[f"psA{oth}"], inc=False)
                for oth in range(4):
                    ot = half * 4 + oth
                    dst = psA[:, oth * 512:oth * 512 + TF]
                    E("T", lambda e, ot=ot, dst=dst: e.matmul(
                        out=dst, lhsT=wsum_bf[:, ot * 128:(ot + 1) * 128],
                        rhs=negm_bf[:], start=False, stop=True),
                      reads=["negm", "wsum"], writes=[f"psA{oth}"])
                    E("V", lambda e, dst=dst, oth=oth: e.tensor_tensor(
                        out=tmp_bf[oth % 2][:, 0:TF], in0=dst, in1=rstbc[:, 0:TF],
                        op=OP.mult),
                      reads=[f"psA{oth}", "rsthl"], writes=[f"tb{oth % 2}"])
                    E("V", lambda e, ot=ot, oth=oth: e.tensor_scalar(
                        out=fo_t[ot][:, 0:TF], in0=tmp_bf[oth % 2][:, 0:TF],
                        scalar1=wlnb_f[:, ot:ot + 1], scalar2=None, op0=OP.add),
                      reads=[f"tb{oth % 2}", "wlnb"], writes=[f"fo{ot}"])
                    if "ffo" in dbg:
                        dma("QG", dbg_out["ffo"][ot * 128:(ot + 1) * 128,
                                                 tf * TF:tf * TF + TF],
                            fo_t[ot][:, 0:TF], [f"fo{ot}"], ["tapffo"])

            def p5_E(tf):
                s = (tf % 2) * DB
                t0 = tf * TF
                tiles = [((lambda k=k: fo_t[k][:, 0:TF]), f"fo{k}") for k in range(DB)]
                kr, kmr = emit_ln(tiles, TF)
                for k in range(DB):
                    E("V", lambda e, k=k: e.tensor_tensor(
                        out=tmp_bf[1][:, 0:TF], in0=fo_t[k][:, 0:TF], in1=rstbc[:, 0:TF],
                        op=OP.mult),
                      reads=[f"fo{k}", kr], writes=["tb1"])
                    E("V", lambda e: e.tensor_tensor(
                        out=tmp_bf[1][:, 0:TF], in0=tmp_bf[1][:, 0:TF],
                        in1=rstbc[:, TF:2 * TF], op=OP.subtract),
                      reads=["tb1", kmr], writes=["tb1"])
                    E("V", lambda e, k=k, s=s: e.tensor_tensor(
                        out=outf[:, 0:TF], in0=x1k[s + k][:], in1=tmp_bf[1][:, 0:TF],
                        op=OP.add),
                      reads=["tb1", f"x1k{s + k}"], writes=["outf"])
                    dma("QG", out_sl[k * 128:(k + 1) * 128, t0:t0 + TF], outf[:, 0:TF],
                        ["outf"], ["OUT"])

            p5_A(0)
            for tf in range(NTF):
                p5_B(tf)
                if tf + 1 < NTF:
                    p5_A(tf + 1)
                p5_C(tf)
                p5_D(tf, 0)
                p5_D(tf, 1)
                p5_E(tf)

        _finish(rec, E, dma)
        _replay(nc, rec)
    return nc, dbg_out


def _finish(rec, E, dma):
    rec.drain("QG", "p")
    rec.drain("QS", "p")
    rec.drain("QA", "p")
    rec.drain("CC", "p")


def _replay(nc, rec):
    sems = {}
    with ExitStack() as ctx:
        pass
    sem_ctx = ExitStack()
    for e in Rec.ENG:
        sems[e] = sem_ctx.enter_context(nc.semaphore(f"sb_{e}"))
    blk_ctx = ExitStack()
    block = blk_ctx.enter_context(nc.Block())

    def replay(stream, eng_obj):
        from contextlib import ExitStack as _ES
        scope_ctx = _ES()
        for item in rec.items[stream]:
            if item[0] == "wait":
                _, other, cval = item
                eng_obj.wait_ge(sems[other], cval)
            elif item[0] == "scope":
                scope_ctx.close()
                scope_ctx = _ES()
                scope_ctx.enter_context(nc.named_scope(item[1]))
            else:
                _, fn, eng, step = item
                inst = fn(eng_obj)
                if eng is not None:
                    inst.then_inc(sems[eng], step)
        scope_ctx.close()

    @block.sync
    def _(sync):
        replay("sync", sync)

    @block.vector
    def _(v):
        replay("v", v)

    @block.scalar
    def _(s):
        replay("s", s)

    @block.tensor
    def _(t):
        replay("t", t)

    @block.gpsimd
    def _(p):
        replay("p", p)

    blk_ctx.close()
    sem_ctx.close()



def make_inputs(inputs):
    x = np.asarray(inputs["L0_attn_x"])[0]
    xT = np.ascontiguousarray(x.T.astype(np.float32))
    w_in_t = np.ascontiguousarray(np.asarray(inputs["L0_attn_W_in"]).T.astype(np.float32))
    w_x_t = np.ascontiguousarray(np.asarray(inputs["L0_attn_W_x"]).T.astype(np.float32))
    w_dt_t = np.ascontiguousarray(np.asarray(inputs["L0_attn_W_dt"]).T.astype(np.float32))
    w_out_t = np.ascontiguousarray(np.asarray(inputs["L0_attn_W_out"]).T.astype(np.float32))
    w_fc_t = np.ascontiguousarray(np.asarray(inputs["L0_ffn_W_fc"]).T.astype(np.float32))
    w_proj_t = np.ascontiguousarray(np.asarray(inputs["L0_ffn_W_proj"]).T.astype(np.float32))
    conv_w2 = np.ascontiguousarray(np.asarray(inputs["L0_attn_conv_w"])[:, 0, :].astype(np.float32))
    b_dt2 = np.ascontiguousarray(np.asarray(inputs["L0_attn_b_dt"]).astype(np.float32).reshape(D, 1))
    a_log = np.ascontiguousarray(np.asarray(inputs["L0_attn_A_log"]).astype(np.float32))
    lnw = np.ascontiguousarray(np.asarray(inputs["L0_ffn_ln_w"]).astype(np.float32).reshape(HD, 1))
    lnb = np.asarray(inputs["L0_ffn_ln_b"]).astype(ml_dtypes.bfloat16).reshape(HD, 1)
    eye = np.eye(128, dtype=ml_dtypes.bfloat16)

    in_maps = []
    for c in range(NCORES):
        t0 = c * T
        x_slc = np.zeros((D, T + HL), dtype=np.float32)
        lo = max(0, t0 - HL)
        x_slc[:, HL - (t0 - lo):] = xT[:, lo:t0 + T]
        mask = np.zeros((128, NCORES), dtype=np.float32)
        mask[:, :c] = 1.0
        im = dict(
            x_sl=x_slc, w_in_t=w_in_t, w_x_t=w_x_t, w_dt_t=w_dt_t,
            w_out_t=w_out_t, w_fc_t=w_fc_t, w_proj_t=w_proj_t,
            conv_w2=conv_w2, b_dt2=b_dt2, a_log=a_log,
            lnw_in=lnw, lnb_in=lnb, eye_in=eye,
            onescb_in=np.ones((128, 1), dtype=ml_dtypes.bfloat16),
            onesrb_in=np.ones((1, 128), dtype=ml_dtypes.bfloat16),
            onescf_in=np.ones((128, 1), dtype=np.float32),
            onesrf_in=np.ones((1, 128), dtype=np.float32),
            maskprev=mask,
            cns_in=np.array([[LNEPS, -0.5]], dtype=np.float32),
            gfc_sl=np.ascontiguousarray(w_fc_t[c * 128:(c + 1) * 128, :]),
            gpr_sl=np.ascontiguousarray(w_proj_t[c * 512:(c + 1) * 512, :]),
        )
        in_maps.append(im)
    return in_maps


def run(inputs, dbg=(), stop_after="full", p1_nchunks=NTC, dma_serial=False, **kw):
    nc, dbg_out = build(dbg=dbg, stop_after=stop_after, p1_nchunks=p1_nchunks, dma_serial=dma_serial)
    in_maps = make_inputs(inputs)
    res = run_bass_kernel_spmd(nc, in_maps, core_ids=list(range(NCORES)), **kw)
    return res


def kernel(**inputs):
    res = run(inputs)
    parts = [np.asarray(res.results[c]["out_sl"], dtype=np.float32) for c in range(NCORES)]
    full = np.concatenate([p.T for p in parts], axis=0)
    return np.ascontiguousarray(full[None, :, :], dtype=np.float32)


# version 20
# speedup vs baseline: 1.0299x; 1.0299x over previous
import numpy as np
import ml_dtypes
import concourse.bass as bass
import concourse.mybir as mybir
from concourse.bass_utils import run_bass_kernel_spmd
from contextlib import ExitStack

F32 = mybir.dt.float32
BF16 = mybir.dt.bfloat16
AF = mybir.ActivationFunctionType
OP = mybir.AluOpType
AX = mybir.AxisListType.X

NCORES = 8
D = 1024
DB = 8
T = 2048
HL = 3
N = 16
HD = 4096
JT = HD // 128
TC = 512
NTC = T // TC
TF = 256
NTF = T // TF
LNEPS = 1e-5
ABSEPS = 1e-6


class Rec:

    NQ = 6
    QUEUES = ("QS", "QA", "QG")
    ENG = ("V", "S", "T", "P", "CC") + tuple(
        f"{q}{j}" for q in ("QS", "QA", "QG") for j in range(6))
    ISSUE = {"V": "v", "S": "s", "T": "t", "P": "p", "CC": "p"}
    for _q, _st in (("QS", "sync"), ("QA", "s"), ("QG", "p")):
        for _j in range(6):
            ISSUE[f"{_q}{_j}"] = _st

    def __init__(self, dma_serial=False):
        self.dma_serial = dma_serial
        self.items = {k: [] for k in ("v", "s", "t", "p", "sync")}
        self.cnt = {e: 0 for e in self.ENG}
        self.qrot = {q: 0 for q in self.QUEUES}
        self.writer = {}
        self.readers = {}
        self.covered = {i: {e: 0 for e in self.ENG} for i in self.items}

    SELF_FENCE = ("V", "S", "P")

    def emit(self, eng, fn, reads=(), writes=(), inc=True, fence=False):
        qthrottle = None
        if eng in self.QUEUES:
            j = self.qrot[eng]
            self.qrot[eng] = (j + 1) % self.NQ
            eng = f"{eng}{j}"
            qthrottle = eng
        stream = self.ISSUE[eng]
        deps = {}
        self_fence = eng in self.SELF_FENCE

        def need(w):
            if w is None:
                return
            weng, wcnt, wfen = w
            if weng != eng or self_fence or wfen:
                deps[weng] = max(deps.get(weng, 0), wcnt)

        for k in reads:
            need(self.writer.get(k))
        for k in writes:
            need(self.writer.get(k))
            for reng, rcnt in self.readers.get(k, {}).items():
                if reng != eng or self_fence:
                    deps[reng] = max(deps.get(reng, 0), rcnt)
        if qthrottle is not None or (self.dma_serial and (eng[:2] in self.QUEUES or eng == "CC")):
            if self.dma_serial:
                for q2 in self.ENG:
                    if q2[:2] in self.QUEUES or q2 == "CC":
                        deps[q2] = max(deps.get(q2, 0), self.cnt[q2])
            elif qthrottle is not None:
                deps[qthrottle] = max(deps.get(qthrottle, 0), self.cnt[qthrottle])

        if eng in deps and deps[eng] > self.cnt[eng]:
            deps[eng] = self.cnt[eng]
        for other, cval in sorted(deps.items()):
            if other == eng and cval <= self.covered[stream].get(f"_self_{eng}", 0):
                continue
            if self.covered[stream][other] < cval or other == eng:
                if other == eng:
                    if cval <= self.covered[stream].get(f"_self_{eng}", 0):
                        continue
                    self.items[stream].append(("wait", other, cval))
                    self.covered[stream][f"_self_{eng}"] = cval
                else:
                    self.items[stream].append(("wait", other, cval))
                    self.covered[stream][other] = cval
        step = 16 if eng[:2] in self.QUEUES else 1
        if inc:
            self.items[stream].append(("inst", fn, eng, step))
            self.cnt[eng] += step
            mark = self.cnt[eng]
        else:
            self.items[stream].append(("inst", fn, None, 0))
            mark = self.cnt[eng] + step
        for k in reads:
            self.readers.setdefault(k, {})[eng] = mark
        for k in writes:
            self.writer[k] = (eng, mark, fence)
            self.readers[k] = {}

    def scope(self, name):
        for stream in self.items:
            self.items[stream].append(("scope", name))

    def full_barrier(self):
        for stream in self.items:
            for e in self.ENG:
                c = self.cnt[e]
                if c and self.covered[stream].get(e, 0) < c:
                    self.items[stream].append(("wait", e, c))
                    self.covered[stream][e] = c

    def drain(self, eng, on_stream):
        engs = [f"{eng}{j}" for j in range(self.NQ)] if eng in self.QUEUES else [eng]
        for e in engs:
            c = self.cnt[e]
            if c and self.covered[on_stream][e] < c:
                self.items[on_stream].append(("wait", e, c))
                self.covered[on_stream][e] = c


def build(dbg=(), stop_after="full", p1_nchunks=NTC, dma_serial=False):
    dbg = set(dbg)
    nc = bass.Bass("TRN2", target_bir_lowering=False, debug=False)

    def dram_in(name, shape, dt=F32):
        return nc.dram_tensor(name, shape, dt, kind="ExternalInput").ap()

    x_sl = dram_in("x_sl", [D, T + HL])
    w_in_t = dram_in("w_in_t", [D, 2 * D])
    w_x_t = dram_in("w_x_t", [D, 2 * N])
    w_dt_t = dram_in("w_dt_t", [N, D])
    w_out_t = dram_in("w_out_t", [D, D])
    w_fc_t = dram_in("w_fc_t", [D, HD])
    w_proj_t = dram_in("w_proj_t", [HD, D])
    conv_w2 = dram_in("conv_w2", [D, 4])
    b_dt2 = dram_in("b_dt2", [D, 1])
    a_log = dram_in("a_log", [D, N])
    lnw_in = dram_in("lnw_in", [HD, 1], F32)
    lnb_in = dram_in("lnb_in", [HD, 1], BF16)
    eye_in = dram_in("eye_in", [128, 128], BF16)
    onescb_in = dram_in("onescb_in", [128, 1], BF16)
    onesrb_in = dram_in("onesrb_in", [1, 128], BF16)
    onescf_in = dram_in("onescf_in", [128, 1], F32)
    onesrf_in = dram_in("onesrf_in", [1, 128], F32)
    maskprev = dram_in("maskprev", [128, NCORES])
    cns_in = dram_in("cns_in", [1, 2])
    gfc_sl = dram_in("gfc_sl", [128, HD])
    gpr_sl = dram_in("gpr_sl", [512, D])

    out_sl = nc.dram_tensor("out_sl", [D, T], F32, kind="ExternalOutput").ap()
    dbg_out = {}

    def tap(name, shape, dt=F32):
        if name in dbg:
            dbg_out[name] = nc.dram_tensor("tap_" + name, shape, dt,
                                           kind="ExternalOutput").ap()
        return dbg_out.get(name)

    sz_dram = nc.dram_tensor("sz_dram", [D, T], BF16, kind="Internal").ap()
    bmat_dram = nc.dram_tensor("bmat_dram", [N, T], BF16, kind="Internal").ap()
    yl_dram = nc.dram_tensor("yl_dram", [D, T], BF16, kind="Internal").ap()
    x1_dram = nc.dram_tensor("x1_dram", [D, T], BF16, kind="Internal").ap()
    cc_send_t = nc.dram_tensor("cc_send", [256, 128], F32)
    cc_recv_t = nc.dram_tensor("cc_recv", [NCORES * 256, 128], F32)
    gam_send_t = nc.dram_tensor("gam_send", [1, 2], F32)
    gam_recv_t = nc.dram_tensor("gam_recv", [1, 2], F32)

    rec = Rec(dma_serial=dma_serial)
    E = rec.emit

    def dma(q, out, in_, reads, writes, inc=True):
        E(q, lambda e: e.dma_start(out=out, in_=in_), reads=reads, writes=writes, inc=inc)

    with ExitStack() as octx:
        def sbuf(ctx, name, shape, dt):
            return ctx.enter_context(nc.sbuf_tensor(name, shape, dt))

        o = octx
        mo_t = [sbuf(o, f"mo{k}", [128, TC], BF16) for k in range(DB)]
        x2r = [sbuf(o, f"x2r{i}", [128, TC], BF16) for i in range(2)]
        rstbc = sbuf(o, "rstbc", [128, 2 * TC], BF16)
        srf = sbuf(o, "srf", [1, 1536], F32)
        srb = sbuf(o, "srb", [1, 1024], BF16)
        outf = sbuf(o, "outf", [128, TF], F32)
        tmp_bf = [sbuf(o, f"tmpbf{i}", [128, TC], BF16) for i in range(2)]
        g2r = [sbuf(o, f"g2r{i}", [128, TF], BF16) for i in range(4)]
        onescb = sbuf(o, "onescb", [128, 1], BF16)
        onesrb = sbuf(o, "onesrb", [1, 128], BF16)
        onescf = sbuf(o, "onescf", [128, 1], F32)
        onesrf = sbuf(o, "onesrf", [1, 128], F32)
        msk = sbuf(o, "msk", [128, NCORES], F32)
        cns = sbuf(o, "cns", [1, 2], F32)
        cw = sbuf(o, "cw", [128, 4 * DB], F32)
        bdt = sbuf(o, "bdt", [128, DB], F32)
        A_sb = [sbuf(o, f"A{k}", [128, N], F32) for k in range(DB)]
        lnw_sb = sbuf(o, "lnw_sb", [128, JT], BF16)
        lnwf_sb = sbuf(o, "lnwf_sb", [128, JT], F32)
        lnbc = [sbuf(o, f"lnbc{j}", [128, 1], BF16) for j in range(JT)]
        gacc = sbuf(o, "gacc", [128, 2], F32)
        gpart = sbuf(o, "gpart", [128, 1], F32)
        gam = sbuf(o, "gam", [1, 2], F32)
        thr = sbuf(o, "thr", [128, 2], F32)
        nthr = sbuf(o, "nthr", [128, 2], F32)
        wsum_bf = sbuf(o, "wsum_bf", [1, D], BF16)
        wlnb_f = sbuf(o, "wlnb_f", [128, DB], F32)
        negm_bf = sbuf(o, "negm_bf", [1, TF], BF16)
        wxdt = [sbuf(o, f"wxdt{k}", [128, N], BF16) for k in range(DB)]
        wxB = [sbuf(o, f"wxB{k}", [128, N], BF16) for k in range(DB)]

        psA = octx.enter_context(nc.psum_tensor("psA", [128, 2048], F32))
        psB = octx.enter_context(nc.psum_tensor("psB", [128, 2048], F32))

        SRM, SRV, SRMR = 0, 512, 1024
        SBR, SBMR = 0, 512

        def emit_ln(tiles, w, x2eng="V", dn=None):
            nt = len(tiles)
            Dn = float(dn if dn is not None else nt * 128)
            for i, (apf, rk) in enumerate(tiles):
                E("T", lambda e, apf=apf, i=i: e.matmul(
                    out=psB[0:1, 1536:1536 + w], lhsT=onescb[:], rhs=apf(),
                    start=(i == 0), stop=(i == nt - 1)),
                  reads=[rk, "onescb"], writes=["psB3"], inc=(i == nt - 1))
            for i, (apf, rk) in enumerate(tiles):
                x2 = x2r[i % 2]
                E(x2eng, lambda e, apf=apf, x2=x2: e.tensor_tensor(
                    out=x2[:, 0:w], in0=apf(), in1=apf(), op=OP.mult),
                  reads=[rk], writes=[f"x2_{i % 2}"])
                E("T", lambda e, i=i, x2=x2: e.matmul(
                    out=psB[0:1, 1536 - w:1536], lhsT=onescb[:], rhs=x2[:, 0:w],
                    start=(i == 0), stop=(i == nt - 1)),
                  reads=[f"x2_{i % 2}", "onescb"], writes=["psB2"], inc=True)
            E("V", lambda e: e.tensor_scalar(out=srf[:, 0:2 * w],
                                             in0=psB[0:1, 1536 - w:1536 + w],
                                             scalar1=1.0 / Dn, scalar2=None, op0=OP.mult),
              reads=["psB3", "psB2"], writes=["sm", "sv"])
            E("V", lambda e: e.tensor_tensor(out=srf[:, 2 * w:3 * w], in0=srf[:, w:2 * w],
                                             in1=srf[:, w:2 * w], op=OP.mult),
              reads=["sm"], writes=["smr"])
            E("V", lambda e: e.tensor_tensor(out=srf[:, 0:w], in0=srf[:, 0:w],
                                             in1=srf[:, 2 * w:3 * w], op=OP.subtract),
              reads=["smr", "sv"], writes=["sv"])
            E("S", lambda e: e.activation(out=srf[:, 0:w], in_=srf[:, 0:w],
                                          func=AF.Ln, bias=cns[0:1, 0:1]),
              reads=["sv", "cns"], writes=["sv"])
            E("S", lambda e: e.activation(out=srb[:, 0:w], in_=srf[:, 0:w],
                                          func=AF.Exp, scale=cns[0:1, 1:2]),
              reads=["sv", "cns"], writes=["srbr"])
            E("V", lambda e: e.tensor_tensor(out=srb[:, w:2 * w], in0=srf[:, w:2 * w],
                                             in1=srb[:, 0:w], op=OP.mult),
              reads=["sm", "srbr"], writes=["srbmr"])
            if 2 * w <= 512:
                E("T", lambda e: e.matmul(out=psB[:, 1024:1024 + 2 * w], lhsT=onesrb[:],
                                          rhs=srb[:, 0:2 * w], start=True, stop=True),
                  reads=["srbr", "srbmr", "onesrb"], writes=["psB2", "psB3"])
            else:
                E("T", lambda e: e.matmul(out=psB[:, 1024:1024 + w], lhsT=onesrb[:],
                                          rhs=srb[:, 0:w], start=True, stop=True),
                  reads=["srbr", "onesrb"], writes=["psB2"])
                E("T", lambda e: e.matmul(out=psB[:, 1024 + w:1024 + 2 * w], lhsT=onesrb[:],
                                          rhs=srb[:, w:2 * w], start=True, stop=True),
                  reads=["srbmr", "onesrb"], writes=["psB3"])
            E("S", lambda e: e.copy(out=rstbc[:, 0:2 * w], in_=psB[:, 1024:1024 + 2 * w]),
              reads=["psB2", "psB3"], writes=["rstbc0", "rstbc1"])
            return "rstbc0", "rstbc1"

        rec.scope("P0")
        dma("QG", onescb[:], onescb_in, ["EXT"], ["onescb"])
        dma("QG", onesrb[:], onesrb_in, ["EXT"], ["onesrb"])
        dma("QG", onescf[:], onescf_in, ["EXT"], ["onescf"])
        dma("QG", onesrf[:], onesrf_in, ["EXT"], ["onesrf"])
        dma("QG", msk[:], maskprev, ["EXT"], ["msk"])
        dma("QG", cns[:], cns_in, ["EXT"], ["cns"])
        for k in range(DB):
            dma("QG", cw[:, 4 * k:4 * k + 4], conv_w2[k * 128:(k + 1) * 128, :], ["EXT"], ["cw"])
            dma("QG", bdt[:, k:k + 1], b_dt2[k * 128:(k + 1) * 128, :], ["EXT"], ["bdt"])
            dma("QG", A_sb[k][:], a_log[k * 128:(k + 1) * 128, :], ["EXT"], [f"A{k}"])
            E("S", lambda e, k=k: e.activation(out=A_sb[k][:], in_=A_sb[k][:], func=AF.Exp),
              reads=[f"A{k}"], writes=[f"A{k}"])
            E("V", lambda e, k=k: e.tensor_scalar(out=A_sb[k][:], in0=A_sb[k][:],
                                                  scalar1=-1.0, scalar2=None, op0=OP.mult),
              reads=[f"A{k}"], writes=[f"A{k}"])
        for j in range(JT):
            dma("QG", lnwf_sb[:, j:j + 1], lnw_in[j * 128:(j + 1) * 128, :], ["EXT"], ["lnwf"])
            dma("QG", lnbc[j][:], lnb_in[j * 128:(j + 1) * 128, :], ["EXT"], [f"lnb{j}"])
        E("S", lambda e: e.copy(out=lnw_sb[:], in_=lnwf_sb[:]), reads=["lnwf"], writes=["lnw"])

        with ExitStack() as actx:
            a = actx
            xsb = [sbuf(a, f"xsb{k}", [128, T + HL], BF16) for k in range(DB)]
            szr = [sbuf(a, f"szr{i}", [128, TC], BF16) for i in range(2)]
            cv = [sbuf(a, f"cv{i}", [128, TC], BF16) for i in range(3)]
            xcar = [sbuf(a, f"xca{k}", [128, T], BF16) for k in range(DB)]
            eye_bf = sbuf(a, "eye_bf", [128, 128], BF16)
            hend = sbuf(a, "hend", [128, 128], F32)
            aprod = sbuf(a, "aprod", [128, 128], F32)
            ap16 = sbuf(a, "ap16", [128, N], F32)
            hrec = [sbuf(a, f"hrec{r}", [128, 128], F32) for r in range(2)]
            h0 = sbuf(a, "h0", [128, 128], F32)
            hP = sbuf(a, "hP", [128, 128], F32)
            wdtb = [sbuf(a, f"wdtb{k}", [N, 128], BF16) for k in range(DB)]
            dtlow = sbuf(a, "dtlow", [N, T], BF16)
            bmr = [sbuf(a, f"bmr{i}", [N, TC], BF16) for i in range(2)]
            dma("QG", eye_bf[:], eye_in, ["EXT"], ["eye"])

            stage2 = [sbuf(a, f"stage{i}", [128, T + HL], F32) for i in range(2)]
            def stg(i):
                return stage2[i % 2]
            si = [0]
            SQ = ("QS", "QA")
            for k in range(DB):
                i = si[0] % 2; si[0] += 1
                dma(SQ[i], stg(i)[:, 0:T + HL], x_sl[k * 128:(k + 1) * 128, :], ["EXT"], [f"stg{i}"])
                E("S", lambda e, k=k, i=i: e.copy(out=xsb[k][:], in_=stg(i)[:, 0:T + HL]),
                  reads=[f"stg{i}"], writes=[f"x{k}"])
            for k in range(DB):
                i = si[0] % 2; si[0] += 1
                dma(SQ[i], stg(i)[:, 0:2 * N], w_x_t[k * 128:(k + 1) * 128, :], ["EXT"], [f"stg{i}"])
                E("S", lambda e, k=k, i=i: e.copy(out=wxdt[k][:], in_=stg(i)[:, 0:N]),
                  reads=[f"stg{i}"], writes=[f"wxdt{k}"])
                E("S", lambda e, k=k, i=i: e.copy(out=wxB[k][:], in_=stg(i)[:, N:2 * N]),
                  reads=[f"stg{i}"], writes=[f"wxB{k}"])
            for k in range(DB):
                i = si[0] % 2; si[0] += 1
                dma(SQ[i], stg(i)[0:N, 0:128], w_dt_t[:, k * 128:(k + 1) * 128], ["EXT"], [f"stg{i}"])
                E("S", lambda e, k=k, i=i: e.copy(out=wdtb[k][:], in_=stg(i)[0:N, 0:128]),
                  reads=[f"stg{i}"], writes=[f"wdtb{k}"])

            gstage = sbuf(a, "gstage", [128, 512], F32)

            def emit_gamma():
                for i in range(8):
                    dma("QG", gstage[:], gfc_sl[:, i * 512:(i + 1) * 512], ["EXT"], ["gstg"])
                    E("V", lambda e: e.tensor_reduce(out=gpart[:], in_=gstage[:], axis=AX,
                                                     op=OP.add, apply_absolute_value=True),
                      reads=["gstg"], writes=["gpart"], fence=True)
                    if i == 0:
                        E("V", lambda e: e.tensor_copy(out=gacc[:, 0:1], in_=gpart[:]),
                          reads=["gpart"], writes=["gacc"])
                    else:
                        E("V", lambda e: e.tensor_tensor(out=gacc[:, 0:1], in0=gacc[:, 0:1],
                                                         in1=gpart[:], op=OP.add),
                          reads=["gpart", "gacc"], writes=["gacc"])
                for i in range(8):
                    dma("QG", gstage[:], gpr_sl[(i // 2) * 128:(i // 2 + 1) * 128, (i % 2) * 512:(i % 2 + 1) * 512], ["EXT"], ["gstg"])
                    E("V", lambda e: e.tensor_reduce(out=gpart[:], in_=gstage[:], axis=AX,
                                                     op=OP.add, apply_absolute_value=True),
                      reads=["gstg"], writes=["gpart"], fence=True)
                    if i == 0:
                        E("V", lambda e: e.tensor_copy(out=gacc[:, 1:2], in_=gpart[:]),
                          reads=["gpart", "gacc"], writes=["gacc"])
                    else:
                        E("V", lambda e: e.tensor_tensor(out=gacc[:, 1:2], in0=gacc[:, 1:2],
                                                         in1=gpart[:], op=OP.add),
                          reads=["gpart", "gacc"], writes=["gacc"])
                E("T", lambda e: e.matmul(out=psB[0:1, 0:2], lhsT=onescf[:], rhs=gacc[:],
                                          start=True, stop=True),
                  reads=["gacc", "onescf"], writes=["psB0"])
                E("V", lambda e: e.tensor_copy(out=gam[:], in_=psB[0:1, 0:2]),
                  reads=["psB0"], writes=["gam"])
                dma("QA", gam_send_t.ap()[:], gam[:], ["gam"], ["gamd"])
                E("CC", lambda e: e.collective_compute(
                    "AllReduce", OP.add, replica_groups=[list(range(NCORES))],
                    ins=[gam_send_t.ap().opt()], outs=[gam_recv_t.ap().opt()]),
                  reads=["gamd"], writes=["gamr"])

            def emit_gamma_thr():
                dma("QS", gam[:], gam_recv_t.ap()[:], ["gamr"], ["gam"])
                E("V", lambda e: e.tensor_scalar(out=gam[:], in0=gam[:],
                                                 scalar1=0.5 / (D * HD), scalar2=0.5 * ABSEPS,
                                                 op0=OP.mult, op1=OP.add),
                  reads=["gam"], writes=["gam"])
                E("T", lambda e: e.matmul(out=psB[:, 4:6], lhsT=onesrf[:], rhs=gam[:],
                                          start=True, stop=True),
                  reads=["gam", "onesrf"], writes=["psB0"])
                E("V", lambda e: e.tensor_copy(out=thr[:], in_=psB[:, 4:6]),
                  reads=["psB0"], writes=["thr"])
                E("V", lambda e: e.tensor_scalar(out=nthr[:], in0=thr[:], scalar1=-1.0,
                                                 scalar2=None, op0=OP.mult),
                  reads=["thr"], writes=["nthr"])
                if tap("thrtap", [128, 2]) is not None:
                    dma("QG", dbg_out["thrtap"], thr[:], ["thr"], ["tapthr"])
                if tap("gamma", [1, 2]) is not None:
                    dma("QG", dbg_out["gamma"], gam[:], ["gam"], ["tapgam"])

            emit_gamma()

            if stop_after == "P0":
                emit_gamma_thr()
                _finish(rec, E, dma)
                _replay(nc, rec)
                return nc, dbg_out

            with ExitStack() as p1ctx:
                winb = [sbuf(p1ctx, f"winb{k}", [128, 2 * D], BF16) for k in range(DB)]
                xin2 = [[sbuf(p1ctx, f"xin{p}_{k}", [128, TC + HL], BF16)
                         for k in range(DB)] for p in range(2)]
                xlnt2 = [[sbuf(p1ctx, f"xlnt{p}_{k}", [128, TC], BF16)
                          for k in range(DB)] for p in range(2)]
                for k in range(DB):
                    i = si[0] % 2; si[0] += 1
                    dma(SQ[i], stg(i)[:, 0:2 * D], w_in_t[k * 128:(k + 1) * 128, :],
                        ["EXT"], [f"stg{i}"])
                    E("S", lambda e, k=k, i=i: e.copy(out=winb[k][:], in_=stg(i)[:, 0:2 * D]),
                      reads=[f"stg{i}"], writes=[f"winb{k}"])

                def p1_stats(col0, w):
                    tiles = [((lambda k=k, col0=col0, w=w: xsb[k][:, col0:col0 + w]), f"x{k}")
                             for k in range(DB)]
                    return emit_ln(tiles, w)

                def p1_norm(par, col0, w, kr, kmr):
                    xl = xlnt2[par]
                    for k in range(DB):
                        E("V", lambda e, k=k, col0=col0, w=w, xl=xl: e.tensor_tensor(
                            out=xl[k][:, 0:w], in0=xsb[k][:, col0:col0 + w],
                            in1=rstbc[:, 0:w], op=OP.mult),
                          reads=[f"x{k}", kr], writes=[f"xln{par}_{k}"])
                        E("V", lambda e, k=k, w=w, xl=xl: e.tensor_tensor(
                            out=xl[k][:, 0:w], in0=xl[k][:, 0:w],
                            in1=rstbc[:, w:2 * w], op=OP.subtract),
                          reads=[f"xln{par}_{k}", kmr], writes=[f"xln{par}_{k}"])

                def p1_bulk(cid, par, col0, w, full):
                    xl, xin = xlnt2[par], xin2[par]
                    ots = list(range(16)) if full else list(range(8))
                    for ot in ots:
                        pcol = (ot % 4) * 512
                        for k in range(DB):
                            E("T", lambda e, ot=ot, k=k, pcol=pcol, w=w, xl=xl: e.matmul(
                                out=psA[:, pcol:pcol + w],
                                lhsT=winb[k][:, ot * 128:(ot + 1) * 128],
                                rhs=xl[k][:, 0:w],
                                start=(k == 0), stop=(k == DB - 1)),
                              reads=[f"xln{par}_{k}", f"winb{k}"], writes=[f"psA{ot % 4}"],
                              inc=(k == DB - 1))
                        if ot < 8:
                            dst0 = HL if full else 0
                            E("S", lambda e, ot=ot, pcol=pcol, w=w, dst0=dst0, xin=xin: e.copy(
                                out=xin[ot][:, dst0:dst0 + w], in_=psA[:, pcol:pcol + w]),
                              reads=[f"psA{ot % 4}"], writes=[f"xinp{par}_{ot}"])
                        else:
                            i = ot % 2
                            E("S", lambda e, pcol=pcol, w=w, i=i: e.activation(
                                out=szr[i][:, 0:w], in_=psA[:, pcol:pcol + w], func=AF.Silu),
                              reads=[f"psA{ot % 4}"], writes=[f"szr{i}"])
                            dma("QA", sz_dram[(ot - 8) * 128:(ot - 7) * 128,
                                              col0 - HL:col0 - HL + w],
                                szr[i][:, 0:w], [f"szr{i}"], [f"szd{ot - 8}"])
                    if not full:
                        return
                    tok0 = col0 - HL
                    for k in range(DB):
                        E("V", lambda e, k=k, w=w, xin=xin: e.tensor_scalar(
                            out=cv[0][:, 0:w], in0=xin[k][:, 0:w],
                            scalar1=cw[:, 4 * k:4 * k + 1], scalar2=None, op0=OP.mult),
                          reads=[f"xinp{par}_{k}", "cw"], writes=["cv0"])
                        E("V", lambda e, k=k, w=w, xin=xin: e.tensor_scalar(
                            out=cv[1][:, 0:w], in0=xin[k][:, 1:1 + w],
                            scalar1=cw[:, 4 * k + 1:4 * k + 2], scalar2=None, op0=OP.mult),
                          reads=[f"xinp{par}_{k}", "cw"], writes=["cv1"])
                        E("V", lambda e, w=w: e.tensor_tensor(
                            out=cv[0][:, 0:w], in0=cv[0][:, 0:w], in1=cv[1][:, 0:w], op=OP.add),
                          reads=["cv0", "cv1"], writes=["cv0"])
                        E("V", lambda e, k=k, w=w, xin=xin: e.tensor_scalar(
                            out=cv[1][:, 0:w], in0=xin[k][:, 2:2 + w],
                            scalar1=cw[:, 4 * k + 2:4 * k + 3], scalar2=None, op0=OP.mult),
                          reads=[f"xinp{par}_{k}", "cw"], writes=["cv1"])
                        E("V", lambda e, k=k, w=w, xin=xin: e.tensor_scalar(
                            out=cv[2][:, 0:w], in0=xin[k][:, 3:3 + w],
                            scalar1=cw[:, 4 * k + 3:4 * k + 4], scalar2=None, op0=OP.mult),
                          reads=[f"xinp{par}_{k}", "cw"], writes=["cv2"])
                        E("V", lambda e, w=w: e.tensor_tensor(
                            out=cv[1][:, 0:w], in0=cv[1][:, 0:w], in1=cv[2][:, 0:w], op=OP.add),
                          reads=["cv1", "cv2"], writes=["cv1"])
                        E("V", lambda e, w=w: e.tensor_tensor(
                            out=cv[0][:, 0:w], in0=cv[0][:, 0:w], in1=cv[1][:, 0:w], op=OP.add),
                          reads=["cv0", "cv1"], writes=["cv0"])
                        E("S", lambda e, k=k, tok0=tok0, w=w: e.activation(
                            out=xcar[k][:, tok0:tok0 + w], in_=cv[0][:, 0:w], func=AF.Silu),
                          reads=["cv0"], writes=[f"xc{k}"])
                        if cid + 1 < NTC:
                            E("V", lambda e, k=k, w=w, xin=xin: e.tensor_copy(
                                out=xin2[par ^ 1][k][:, 0:HL], in_=xin[k][:, w:w + HL]),
                              reads=[f"xinp{par}_{k}"], writes=[f"xinp{par ^ 1}_{k}"])

                def p1_wx(cid, col0, w):
                    tok0 = col0 - HL
                    for k in range(DB):
                        E("T", lambda e, k=k, tok0=tok0, w=w: e.matmul(
                            out=psB[0:N, 0:w], lhsT=wxdt[k][:],
                            rhs=xcar[k][:, tok0:tok0 + w],
                            start=(k == 0), stop=(k == DB - 1)),
                          reads=[f"xc{k}", f"wxdt{k}"], writes=["psB0"], inc=(k == DB - 1))
                    E("S", lambda e, tok0=tok0, w=w: e.copy(
                        out=dtlow[:, tok0:tok0 + w], in_=psB[0:N, 0:w]),
                      reads=["psB0"], writes=["dtlow"])
                    for k in range(DB):
                        E("T", lambda e, k=k, tok0=tok0, w=w: e.matmul(
                            out=psB[0:N, 512:512 + w], lhsT=wxB[k][:],
                            rhs=xcar[k][:, tok0:tok0 + w],
                            start=(k == 0), stop=(k == DB - 1)),
                          reads=[f"xc{k}", f"wxB{k}"], writes=["psB1"], inc=(k == DB - 1))
                    i = cid % 2
                    E("S", lambda e, i=i, w=w: e.copy(out=bmr[i][:, 0:w],
                                                      in_=psB[0:N, 512:512 + w]),
                      reads=["psB1"], writes=[f"bmr{i}"])
                    dma("QA", bmat_dram[:, tok0:tok0 + w], bmr[i][:, 0:w],
                        [f"bmr{i}"], ["bmatd"])

                if p1_nchunks >= 0:
                    rec.scope("P1")
                kr, kmr = p1_stats(0, HL)
                p1_norm(0, 0, HL, kr, kmr)
                p1_bulk(-1, 0, 0, HL, full=False)
                nch = max(p1_nchunks, 0)
                if nch:
                    kr, kmr = p1_stats(HL, TC)
                    p1_norm(0, HL, TC, kr, kmr)
                for tc in range(nch):
                    if tc + 1 < nch:
                        krn, kmrn = p1_stats(HL + (tc + 1) * TC, TC)
                    p1_bulk(tc, tc % 2, HL + tc * TC, TC, full=True)
                    if tc > 0:
                        p1_wx(tc - 1, HL + (tc - 1) * TC, TC)
                    if tc + 1 < nch:
                        p1_norm((tc + 1) % 2, HL + (tc + 1) * TC, TC, krn, kmrn)
                if nch:
                    p1_wx(nch - 1, HL + (nch - 1) * TC, TC)


                if tap("xc", [D, T]) is not None:
                    for k in range(DB):
                        dma("QG", dbg_out["xc"][k * 128:(k + 1) * 128, :], xcar[k][:],
                            [f"xc{k}"], ["tapxc"])
                if tap("dtlow", [N, T]) is not None:
                    dma("QG", dbg_out["dtlow"].bitcast(BF16)[:, 0:T], dtlow[:],
                        ["dtlow"], ["tapdtl"]) if False else \
                        dma("QG", dbg_out["dtlow"], dtlow[:], ["dtlow"], ["tapdtl"])

            if stop_after == "P1":
                _finish(rec, E, dma)
                _replay(nc, rec)
                return nc, dbg_out

            rec.scope("P2")
            rec.full_barrier()
            with ExitStack() as p2ctx:
                dt_t = sbuf(p2ctx, "dt_t", [128, T], F32)
                sdt_t = sbuf(p2ctx, "sdt_t", [128, T], F32)
                softr = sdt_t
                softt = sbuf(p2ctx, "softt", [128, T], F32)
                a_t = [sbuf(p2ctx, f"a_t{i}", [128, T], F32) for i in range(3)]
                bc_t = [sbuf(p2ctx, f"bc_t{i}", [128, T], BF16) for i in range(2)]
                b_t = [sbuf(p2ctx, f"b_t{i}", [128, T], BF16) for i in range(2)]
                h_t = [sbuf(p2ctx, f"h_t{i}", [128, T], BF16) for i in range(3)]
                yst = [sbuf(p2ctx, f"yst{i}", [128, TC], BF16) for i in range(2)]

                def emit_dt(k):
                    for tc4 in range(NTC):
                        E("T", lambda e, k=k, tc4=tc4: e.matmul(
                            out=psB[:, tc4 * TC:(tc4 + 1) * TC], lhsT=wdtb[k][:],
                            rhs=dtlow[:, tc4 * TC:(tc4 + 1) * TC], start=True, stop=True),
                          reads=["dtlow", f"wdtb{k}"], writes=[f"psB{tc4}"])
                    E("S", lambda e, k=k: e.activation(out=softr[:], in_=psB[:, 0:T],
                                                       func=AF.Relu, bias=bdt[:, k:k + 1]),
                      reads=["psB0", "psB1", "psB2", "psB3", "bdt"],
                      writes=["softr", "sdt"])
                    E("S", lambda e, k=k: e.activation(out=softt[:], in_=psB[:, 0:T],
                                                       func=AF.Abs, bias=bdt[:, k:k + 1]),
                      reads=["psB0", "psB1", "psB2", "psB3", "bdt"], writes=["softt"],
                      fence=True)
                    E("S", lambda e: e.activation(out=softt[:], in_=softt[:], func=AF.Exp,
                                                  scale=-1.0),
                      reads=["softt"], writes=["softt"], fence=True)
                    E("S", lambda e: e.activation(out=softt[:], in_=softt[:], func=AF.Ln,
                                                  bias=1.0),
                      reads=["softt"], writes=["softt"], fence=True)
                    E("V", lambda e: e.tensor_tensor(out=dt_t[:], in0=softr[:], in1=softt[:],
                                                     op=OP.add),
                      reads=["softr", "softt"], writes=["dt"])
                    E("V", lambda e: e.tensor_tensor_scan(out=sdt_t[:], data0=dt_t[:],
                                                          data1=dt_t[:], initial=0.0,
                                                          op0=OP.add, op1=OP.max),
                      reads=["dt", "softr"], writes=["sdt", "softr"], fence=True)

                for k in range(DB):
                    emit_dt(k)
                    E("V", lambda e, k=k: e.tensor_scalar(
                        out=ap16[:], in0=A_sb[k][:], scalar1=sdt_t[:, T - 1:T],
                        scalar2=None, op0=OP.mult),
                      reads=[f"A{k}", "sdt"], writes=["ap16"])
                    E("S", lambda e, k=k: e.activation(out=aprod[:, k * N:(k + 1) * N],
                                                       in_=ap16[:], func=AF.Exp),
                      reads=["ap16"], writes=["aprod"])
                    for n in range(N):
                        i = n % 3
                        ib = n % 2
                        ih = n % 3
                        ln = k * N + n
                        E("S", lambda e, k=k, n=n, i=i: e.activation(
                            out=a_t[i][:], in_=dt_t[:], func=AF.Exp,
                            scale=A_sb[k][:, n:n + 1]),
                          reads=["dt", f"A{k}"], writes=[f"a{i}"])
                        dma("QS", bc_t[ib][:],
                            bmat_dram[n:n + 1, :].partition_broadcast(128).squeeze(1),
                            ["bmatd"], [f"bcl{ib}"])
                        E("V", lambda e, k=k, ib=ib: e.tensor_tensor(
                            out=b_t[ib][:], in0=bc_t[ib][:], in1=xcar[k][:], op=OP.mult),
                          reads=[f"bcl{ib}", f"xc{k}"], writes=[f"b{ib}"])
                        E("V", lambda e, i=i, ib=ib: e.tensor_tensor_scan(
                            out=h_t[i][:], data0=a_t[i][:], data1=b_t[ib][:], initial=0.0,
                            op0=OP.mult, op1=OP.add),
                          reads=[f"a{i}", f"b{ib}"], writes=[f"h{i}"], fence=True)
                        E("V", lambda e, i=i, ln=ln: e.tensor_copy(
                            out=hend[:, ln:ln + 1], in_=h_t[i][:, T - 1:T]),
                          reads=[f"h{i}"], writes=["hend"])
                        for tc4 in range(NTC):
                            E("T", lambda e, i=i, tc4=tc4, n=n: e.matmul(
                                out=psA[:, tc4 * TC:(tc4 + 1) * TC], lhsT=eye_bf[:],
                                rhs=h_t[i][:, tc4 * TC:(tc4 + 1) * TC],
                                start=(n == 0), stop=(n == N - 1)),
                              reads=[f"h{i}", "eye"], writes=[f"psA{tc4}"],
                              inc=(tc4 == NTC - 1))
                    for tc4 in range(NTC):
                        j = tc4 % 2
                        E("S", lambda e, tc4=tc4, j=j: e.copy(
                            out=yst[j][:], in_=psA[:, tc4 * TC:(tc4 + 1) * TC]),
                          reads=[f"psA{tc4}"], writes=[f"yst{j}"])
                        dma("QA", yl_dram[k * 128:(k + 1) * 128, tc4 * TC:(tc4 + 1) * TC],
                            yst[j][:], [f"yst{j}"], ["yld"])

                if tap("yloc", [D, T], BF16) is not None:
                    rec.drain("QG", "p")
                    dma("QG", dbg_out["yloc"], yl_dram, ["yld"], ["tapyl"])
                if tap("hend", [128, 128]) is not None:
                    dma("QG", dbg_out["hend"], hend[:], ["hend"], ["taphe"])
                if tap("aprod", [128, 128]) is not None:
                    dma("QG", dbg_out["aprod"], aprod[:], ["aprod"], ["tapap"])

                if stop_after == "P2":
                    _finish(rec, E, dma)
                    _replay(nc, rec)
                    return nc, dbg_out

                rec.scope("P3")
                dma("QA", cc_send_t.ap()[0:128, :], hend[:], ["hend"], ["ccs"])
                dma("QA", cc_send_t.ap()[128:256, :], aprod[:], ["aprod"], ["ccs"])
                E("CC", lambda e: e.collective_compute(
                    "AllGather", OP.bypass, replica_groups=[list(range(NCORES))],
                    ins=[cc_send_t.ap().opt()], outs=[cc_recv_t.ap().opt()]),
                  reads=["ccs"], writes=["ccr"])
                E("V", lambda e: e.memset(h0[:], 0.0), writes=["h0"])
                for j in range(NCORES):
                    dma("QS", hrec[0][:], cc_recv_t.ap()[j * 256:j * 256 + 128, :],
                        ["ccr"], ["hr0"])
                    dma("QS", hrec[1][:], cc_recv_t.ap()[j * 256 + 128:j * 256 + 256, :],
                        ["ccr"], ["hr1"])
                    E("V", lambda e, j=j: e.tensor_scalar(
                        out=hP[:], in0=hrec[0][:], scalar1=msk[:, j:j + 1], scalar2=None,
                        op0=OP.mult),
                      reads=["hr0", "msk"], writes=["hP"])
                    E("V", lambda e, j=j: e.tensor_scalar(
                        out=hrec[1][:], in0=hrec[1][:], scalar1=1.0, scalar2=msk[:, j:j + 1],
                        op0=OP.subtract, op1=OP.mult),
                      reads=["hr1", "msk"], writes=["hr1"])
                    E("V", lambda e: e.tensor_scalar(
                        out=hrec[1][:], in0=hrec[1][:], scalar1=1.0, scalar2=None, op0=OP.add),
                      reads=["hr1"], writes=["hr1"])
                    E("V", lambda e: e.tensor_tensor(out=h0[:], in0=h0[:], in1=hrec[1][:],
                                                     op=OP.mult),
                      reads=["h0", "hr1"], writes=["h0"])
                    E("V", lambda e: e.tensor_tensor(out=h0[:], in0=h0[:], in1=hP[:],
                                                     op=OP.add),
                      reads=["h0", "hP"], writes=["h0"])
                if tap("h0", [128, 128]) is not None:
                    dma("QG", dbg_out["h0"], h0[:], ["h0"], ["taph0"])

            rec.scope("P4a")
            rec.full_barrier()
            with ExitStack() as p4ctx:
                dt_t = sbuf(p4ctx, "dt4", [128, T], F32)
                sdt_t = sbuf(p4ctx, "sdt4", [128, T], F32)
                softr = sdt_t
                softt = sbuf(p4ctx, "softt4", [128, T], F32)
                cuma = [sbuf(p4ctx, f"cuma{i}", [128, T], BF16) for i in range(3)]
                diag = [sbuf(p4ctx, f"diag{i}", [128, 128], BF16) for i in range(3)]
                szk = [sbuf(p4ctx, f"szk{i}", [128, T], BF16) for i in range(2)]
                ylk = [sbuf(p4ctx, "ylk0", [128, T], BF16)]
                woutb = [sbuf(p4ctx, f"woutb{k}", [128, D], BF16) for k in range(DB)]
                x1st = [sbuf(p4ctx, f"x1st{i}", [128, TC], BF16) for i in range(2)]

                for k in range(DB):
                    i = si[0] % 2; si[0] += 1
                    dma("QS", stg(i)[:, 0:D], w_out_t[k * 128:(k + 1) * 128, :],
                        ["EXT"], [f"stg{i}"])
                    E("S", lambda e, k=k, i=i: e.copy(out=woutb[k][:], in_=stg(i)[:, 0:D]),
                      reads=[f"stg{i}"], writes=[f"woutb{k}"])

                def emit_dt4(k):
                    for tc4 in range(NTC):
                        E("T", lambda e, k=k, tc4=tc4: e.matmul(
                            out=psB[:, tc4 * TC:(tc4 + 1) * TC], lhsT=wdtb[k][:],
                            rhs=dtlow[:, tc4 * TC:(tc4 + 1) * TC], start=True, stop=True),
                          reads=["dtlow", f"wdtb{k}"], writes=[f"psB{tc4}"])
                    E("S", lambda e, k=k: e.activation(out=softr[:], in_=psB[:, 0:T],
                                                       func=AF.Relu, bias=bdt[:, k:k + 1]),
                      reads=["psB0", "psB1", "psB2", "psB3", "bdt"],
                      writes=["softr", "sdt"])
                    E("S", lambda e, k=k: e.activation(out=softt[:], in_=psB[:, 0:T],
                                                       func=AF.Abs, bias=bdt[:, k:k + 1]),
                      reads=["psB0", "psB1", "psB2", "psB3", "bdt"], writes=["softt"],
                      fence=True)
                    E("S", lambda e: e.activation(out=softt[:], in_=softt[:], func=AF.Exp,
                                                  scale=-1.0),
                      reads=["softt"], writes=["softt"], fence=True)
                    E("S", lambda e: e.activation(out=softt[:], in_=softt[:], func=AF.Ln,
                                                  bias=1.0),
                      reads=["softt"], writes=["softt"], fence=True)
                    E("V", lambda e: e.tensor_tensor(out=dt_t[:], in0=softr[:], in1=softt[:],
                                                     op=OP.add),
                      reads=["softr", "softt"], writes=["dt"])
                    E("V", lambda e: e.tensor_tensor_scan(out=sdt_t[:], data0=dt_t[:],
                                                          data1=dt_t[:], initial=0.0,
                                                          op0=OP.add, op1=OP.max),
                      reads=["dt", "softr"], writes=["sdt", "softr"], fence=True)

                for k in range(DB):
                    emit_dt4(k)
                    i = k % 2
                    dma("QA", szk[i][:], sz_dram[k * 128:(k + 1) * 128, :],
                        [f"szd{k}"], [f"szk{i}"])
                    dma("QA", ylk[0][:], yl_dram[k * 128:(k + 1) * 128, :],
                        ["yld"], ["ylk0"])
                    for tc4 in range(NTC):
                        E("T", lambda e, tc4=tc4: e.matmul(
                            out=psA[:, tc4 * TC:(tc4 + 1) * TC], lhsT=eye_bf[:],
                            rhs=ylk[0][:, tc4 * TC:(tc4 + 1) * TC],
                            start=True, stop=False),
                          reads=["ylk0", "eye"], writes=[f"psA{tc4}"], inc=False)
                    for n in range(N):
                        j = n % 3
                        ln = k * N + n
                        E("S", lambda e, k=k, n=n, j=j: e.activation(
                            out=cuma[j][:], in_=sdt_t[:], func=AF.Exp,
                            scale=A_sb[k][:, n:n + 1]),
                          reads=["sdt", f"A{k}"], writes=[f"cu{j}"])
                        E("V", lambda e, j=j, ln=ln: e.tensor_scalar(
                            out=diag[j][:], in0=eye_bf[:], scalar1=h0[:, ln:ln + 1],
                            scalar2=None, op0=OP.mult),
                          reads=["eye", "h0"], writes=[f"dg{j}"])
                        for tc4 in range(NTC):
                            E("T", lambda e, j=j, tc4=tc4, n=n: e.matmul(
                                out=psA[:, tc4 * TC:(tc4 + 1) * TC], lhsT=diag[j][:],
                                rhs=cuma[j][:, tc4 * TC:(tc4 + 1) * TC],
                                start=False, stop=(n == N - 1)),
                              reads=[f"cu{j}", f"dg{j}"], writes=[f"psA{tc4}"],
                              inc=(tc4 == NTC - 1))
                    for tc4 in range(NTC):
                        E("V", lambda e, k=k, i=i, tc4=tc4: e.tensor_tensor(
                            out=xcar[k][:, tc4 * TC:(tc4 + 1) * TC],
                            in0=psA[:, tc4 * TC:(tc4 + 1) * TC],
                            in1=szk[i][:, tc4 * TC:(tc4 + 1) * TC], op=OP.mult),
                          reads=[f"psA{tc4}", f"szk{i}"], writes=[f"g{k}", f"xc{k}"])
                if tap("gated", [D, T]) is not None:
                    for k in range(DB):
                        dma("QG", dbg_out["gated"][k * 128:(k + 1) * 128, :], xcar[k][:],
                            [f"g{k}"], ["tapgt"])

                rec.scope("P4a2")
                for tc in range(NTC):
                    for ot in range(DB):
                        pcol = (ot % 2) * 512
                        for k in range(DB):
                            E("T", lambda e, ot=ot, k=k, pcol=pcol, tc=tc: e.matmul(
                                out=psA[:, pcol:pcol + TC],
                                lhsT=woutb[k][:, ot * 128:(ot + 1) * 128],
                                rhs=xcar[k][:, tc * TC:(tc + 1) * TC],
                                start=(k == 0), stop=(k == DB - 1)),
                              reads=[f"g{k}", f"woutb{k}"], writes=[f"psA{ot % 2}"],
                              inc=(k == DB - 1))
                        E("S", lambda e, ot=ot, pcol=pcol: e.copy(
                            out=mo_t[ot][:], in_=psA[:, pcol:pcol + TC]),
                          reads=[f"psA{ot % 2}"], writes=[f"mo{ot}"])
                    tiles = [((lambda k=k: mo_t[k][:]), f"mo{k}") for k in range(DB)]
                    kr, kmr = emit_ln(tiles, TC)
                    for k in range(DB):
                        j = k % 2
                        E("V", lambda e, k=k, j=j: e.tensor_tensor(
                            out=tmp_bf[j][:], in0=mo_t[k][:], in1=rstbc[:, 0:TC], op=OP.mult),
                          reads=[f"mo{k}", kr], writes=[f"tb{j}"])
                        E("V", lambda e, j=j: e.tensor_tensor(
                            out=tmp_bf[j][:], in0=tmp_bf[j][:], in1=rstbc[:, TC:2 * TC],
                            op=OP.subtract),
                          reads=[f"tb{j}", kmr], writes=[f"tb{j}"])
                        c0 = HL + tc * TC
                        E("V", lambda e, k=k, j=j, c0=c0: e.tensor_tensor(
                            out=x1st[j][:], in0=xsb[k][:, c0:c0 + TC], in1=tmp_bf[j][:],
                            op=OP.add),
                          reads=[f"tb{j}", f"x{k}"], writes=[f"x1st{j}"])
                        dma("QA", x1_dram[k * 128:(k + 1) * 128, tc * TC:(tc + 1) * TC],
                            x1st[j][:], [f"x1st{j}"], ["x1d"])
                if tap("x1", [D, T], BF16) is not None:
                    rec.drain("QG", "p")
                    dma("QG", dbg_out["x1"], x1_dram, ["x1d"], ["tapx1"])

            if stop_after == "P4a":
                _finish(rec, E, dma)
                _replay(nc, rec)
                return nc, dbg_out

        rec.scope("P4b")
        rec.full_barrier()
        emit_gamma_thr()
        with ExitStack() as p5ctx:
            wfcq = [sbuf(p5ctx, f"wfcq{k}", [128, HD], BF16) for k in range(DB)]
            wprq = [sbuf(p5ctx, f"wprq{j}", [128, D], BF16) for j in range(JT)]
            NGW = 6
            gwst = [sbuf(p5ctx, f"gwst{i}", [128, 256], F32) for i in range(NGW)]
            qge = sbuf(p5ctx, "qge", [128, 256], BF16)
            qle = sbuf(p5ctx, "qle", [128, 256], BF16)
            x1k = [sbuf(p5ctx, f"x1k{i}", [128, TF], BF16) for i in range(2 * DB)]
            xl2 = [sbuf(p5ctx, f"xl2{k}", [128, TF], BF16) for k in range(2 * DB)]
            gtl = [sbuf(p5ctx, f"gtl{i}", [128, TF], BF16) for i in range(JT)]
            fo_t = mo_t

            def quant_piece(i, dst, dkey, thrcol, nthrcol):
                E("V", lambda e: e.tensor_scalar(
                    out=qge[:], in0=gwst[i][:], scalar1=thrcol, scalar2=None,
                    op0=OP.is_ge),
                  reads=[f"gq{i}", "thr"], writes=["qge"])
                E("V", lambda e: e.tensor_scalar(
                    out=qle[:], in0=gwst[i][:], scalar1=nthrcol, scalar2=None,
                    op0=OP.is_le),
                  reads=[f"gq{i}", "nthr"], writes=["qle"])
                E("V", lambda e: e.tensor_tensor(
                    out=dst, in0=qge[:], in1=qle[:], op=OP.subtract),
                  reads=["qge", "qle"], writes=[dkey])
            pcnt = [0]
            QROT = ("QS", "QA", "QG")
            for k in range(DB):
                for piece in range(16):
                    i = pcnt[0] % NGW
                    q = QROT[pcnt[0] % 3]
                    pcnt[0] += 1
                    dma(q, gwst[i][:],
                        w_fc_t[k * 128:(k + 1) * 128, piece * 256:(piece + 1) * 256],
                        ["EXT"], [f"gq{i}"])
                    quant_piece(i, wfcq[k][:, piece * 256:(piece + 1) * 256],
                                f"wfcq{k}", thr[:, 0:1], nthr[:, 0:1])
            for j in range(JT):
                for piece in range(4):
                    i = pcnt[0] % NGW
                    q = QROT[pcnt[0] % 3]
                    pcnt[0] += 1
                    dma(q, gwst[i][:],
                        w_proj_t[j * 128:(j + 1) * 128, piece * 256:(piece + 1) * 256],
                        ["EXT"], [f"gq{i}"])
                    quant_piece(i, wprq[j][:, piece * 256:(piece + 1) * 256],
                                f"wprq{j}", thr[:, 1:2], nthr[:, 1:2])
                for ot in range(DB):
                    dst = (psA[:, ot * 512:ot * 512 + 1] if ot < 4
                           else psB[:, (ot - 4) * 512:(ot - 4) * 512 + 1])
                    bkey = (f"psA{ot}" if ot < 4 else f"psB{ot - 4}")
                    E("T", lambda e, j=j, ot=ot, dst=dst: e.matmul(
                        out=dst, lhsT=wprq[j][:, ot * 128:(ot + 1) * 128], rhs=lnbc[j][:],
                        start=(j == 0), stop=(j == JT - 1)),
                      reads=[f"wprq{j}", f"lnb{j}"], writes=[bkey],
                      inc=(ot == DB - 1))
                E("V", lambda e, j=j: e.tensor_scalar(
                    out=wprq[j][:], in0=wprq[j][:], scalar1=lnwf_sb[:, j:j + 1],
                    scalar2=None, op0=OP.mult),
                  reads=[f"wprq{j}", "lnwf"], writes=[f"wprq{j}"])
            E("V", lambda e: e.tensor_copy(out=wlnb_f[:, 0:4], in_=psA[:, 0:1537:512]),
              reads=["psA0", "psA1", "psA2", "psA3"], writes=["wlnb"])
            E("V", lambda e: e.tensor_copy(out=wlnb_f[:, 4:8], in_=psB[:, 0:1537:512]),
              reads=["psB0", "psB1", "psB2", "psB3"], writes=["wlnb"])
            for j in range(JT):
                E("T", lambda e, j=j: e.matmul(
                    out=psB[0:1, 0:512], lhsT=onescb[:],
                    rhs=wprq[j][:, 0:512], start=(j == 0), stop=(j == JT - 1)),
                  reads=[f"wprq{j}", "onescb", "wlnb"], writes=["psB0"], inc=False)
                E("T", lambda e, j=j: e.matmul(
                    out=psB[0:1, 512:1024], lhsT=onescb[:],
                    rhs=wprq[j][:, 512:1024], start=(j == 0), stop=(j == JT - 1)),
                  reads=[f"wprq{j}", "onescb", "wlnb"], writes=["psB1"],
                  inc=(j == JT - 1))
            E("V", lambda e: e.tensor_copy(out=wsum_bf[:], in_=psB[0:1, 0:D]),
              reads=["psB0", "psB1"], writes=["wsum"])
            for nm in ("psB0", "psB1", "psB2", "psB3"):
                pass

            if tap("wlnb", [128, DB]) is not None:
                dma("QG", dbg_out["wlnb"], wlnb_f[:], ["wlnb"], ["tapwlnb"])
            if tap("wqfc", [128, HD], BF16) is not None:
                dma("QG", dbg_out["wqfc"], wfcq[0][:], ["wfcq0"], ["tapwqfc"])
            if tap("wqpr", [128, D], BF16) is not None:
                dma("QG", dbg_out["wqpr"], wprq[0][:], ["wprq0"], ["tapwqpr"])
            if tap("wsum", [1, D]) is not None:
                rec.drain("V", "p")
                E("QG", lambda e: e.dma_start(out=dbg_out["wsum"],
                                              in_=wsum_bf[:].bitcast(BF16)),
                  reads=["wsum"], writes=["tapws"]) if False else \
                    dma("QG", dbg_out["wsum"].bitcast(BF16)[:, 0:D], wsum_bf[:],
                        ["wsum"], ["tapws"])

            tap("ffo", [D, T], BF16)
            rec.scope("P5")

            def p5_A(tf):
                s = (tf % 2) * DB
                t0 = tf * TF
                for k in range(DB):
                    dma("QA", x1k[s + k][:], x1_dram[k * 128:(k + 1) * 128, t0:t0 + TF],
                        ["x1d"], [f"x1k{s + k}"])
                tiles = [((lambda k=k, s=s: x1k[s + k][:]), f"x1k{s + k}") for k in range(DB)]
                kr, kmr = emit_ln(tiles, TF)
                for k in range(DB):
                    E("V", lambda e, k=k, s=s: e.tensor_tensor(
                        out=xl2[s + k][:], in0=x1k[s + k][:], in1=rstbc[:, 0:TF], op=OP.mult),
                      reads=[f"x1k{s + k}", kr], writes=[f"xl2_{s + k}"])
                    E("V", lambda e, k=k, s=s: e.tensor_tensor(
                        out=xl2[s + k][:], in0=xl2[s + k][:], in1=rstbc[:, TF:2 * TF],
                        op=OP.subtract),
                      reads=[f"xl2_{s + k}", kmr], writes=[f"xl2_{s + k}"])

            def p5_B(tf):
                s = (tf % 2) * DB

                for j in range(JT):
                    pcol = (j % 2) * 512
                    for k in range(DB):
                        E("T", lambda e, j=j, k=k, pcol=pcol, s=s: e.matmul(
                            out=psB[:, pcol:pcol + TF],
                            lhsT=wfcq[k][:, j * 128:(j + 1) * 128], rhs=xl2[s + k][:],
                            start=(k == 0), stop=(k == DB - 1)),
                          reads=[f"xl2_{s + k}", f"wfcq{k}"], writes=[f"psB{j % 2}"],
                          inc=(k == DB - 1))
                    gt = gtl[j]
                    E("S", lambda e, j=j, pcol=pcol, gt=gt: e.activation(
                        out=gt[:], in_=psB[:, pcol:pcol + TF], func=AF.Gelu),
                      reads=[f"psB{j % 2}"], writes=[f"gt{j}"])
                for j in range(JT):
                    E("V", lambda e, j=j: e.tensor_tensor(
                        out=g2r[j % 4][:], in0=gtl[j][:], in1=gtl[j][:], op=OP.mult),
                      reads=[f"gt{j}"], writes=[f"g2_{j % 4}"])
                    E("T", lambda e, j=j: e.matmul(
                        out=psA[0:1, 0:TF], lhsT=onescb[:], rhs=g2r[j % 4][:],
                        start=(j == 0), stop=(j == JT - 1)),
                      reads=[f"g2_{j % 4}", "onescb"], writes=["psA0"], inc=True)

            def p5_C(tf):
                for j in range(JT):
                    E("T", lambda e, j=j: e.matmul(
                        out=psB[0:1, 1536:1536 + TF], lhsT=onescb[:], rhs=gtl[j][:],
                        start=(j == 0), stop=(j == JT - 1)),
                      reads=[f"gt{j}", "onescb"], writes=["psB3"],
                      inc=(j == JT - 1))
                E("V", lambda e: e.tensor_scalar(out=srf[:, 0:TF],
                                                 in0=psA[0:1, 0:TF],
                                                 scalar1=1.0 / HD, scalar2=None, op0=OP.mult),
                  reads=["psA0"], writes=["sv"])
                E("V", lambda e: e.tensor_scalar(out=srf[:, TF:2 * TF],
                                                 in0=psB[0:1, 1536:1536 + TF],
                                                 scalar1=1.0 / HD, scalar2=None, op0=OP.mult),
                  reads=["psB3"], writes=["sm"])
                E("V", lambda e: e.tensor_tensor(out=srf[:, 2 * TF:3 * TF],
                                                 in0=srf[:, TF:2 * TF],
                                                 in1=srf[:, TF:2 * TF], op=OP.mult),
                  reads=["sm"], writes=["smr"])
                E("V", lambda e: e.tensor_tensor(out=srf[:, 0:TF], in0=srf[:, 0:TF],
                                                 in1=srf[:, 2 * TF:3 * TF], op=OP.subtract),
                  reads=["smr", "sv"], writes=["sv"])
                E("S", lambda e: e.activation(out=srf[:, 0:TF], in_=srf[:, 0:TF],
                                              func=AF.Ln, bias=cns[0:1, 0:1]),
                  reads=["sv", "cns"], writes=["sv"])
                E("S", lambda e: e.activation(out=srb[:, 0:TF], in_=srf[:, 0:TF],
                                              func=AF.Exp, scale=cns[0:1, 1:2]),
                  reads=["sv", "cns"], writes=["srbr"])
                E("V", lambda e: e.tensor_scalar(out=srf[:, 2 * TF:3 * TF],
                                                 in0=srf[:, TF:2 * TF], scalar1=-1.0,
                                                 scalar2=None, op0=OP.mult),
                  reads=["sm"], writes=["smr"])
                E("V", lambda e: e.tensor_copy(out=negm_bf[:], in_=srf[:, 2 * TF:3 * TF]),
                  reads=["smr"], writes=["negm"])
                E("T", lambda e: e.matmul(out=psB[:, 1024:1024 + TF], lhsT=onesrb[:],
                                          rhs=srb[:, 0:TF], start=True, stop=True),
                  reads=["srbr", "onesrb"], writes=["psB2"])
                E("S", lambda e: e.copy(out=rstbc[:, 0:TF], in_=psB[:, 1024:1024 + TF]),
                  reads=["psB2"], writes=["rsthl"])

            def p5_D(tf, half):
                for j in range(JT):
                    for oth in range(4):
                        ot = half * 4 + oth
                        dst = psA[:, oth * 512:oth * 512 + TF]
                        E("T", lambda e, j=j, ot=ot, dst=dst: e.matmul(
                            out=dst, lhsT=wprq[j][:, ot * 128:(ot + 1) * 128],
                            rhs=gtl[j][:], start=(j == 0), stop=False),
                          reads=[f"gt{j}", f"wprq{j}"], writes=[f"psA{oth}"], inc=False)
                for oth in range(4):
                    ot = half * 4 + oth
                    dst = psA[:, oth * 512:oth * 512 + TF]
                    E("T", lambda e, ot=ot, dst=dst: e.matmul(
                        out=dst, lhsT=wsum_bf[:, ot * 128:(ot + 1) * 128],
                        rhs=negm_bf[:], start=False, stop=True),
                      reads=["negm", "wsum"], writes=[f"psA{oth}"])
                    E("V", lambda e, dst=dst, oth=oth: e.tensor_tensor(
                        out=tmp_bf[oth % 2][:, 0:TF], in0=dst, in1=rstbc[:, 0:TF],
                        op=OP.mult),
                      reads=[f"psA{oth}", "rsthl"], writes=[f"tb{oth % 2}"])
                    E("V", lambda e, ot=ot, oth=oth: e.tensor_scalar(
                        out=fo_t[ot][:, 0:TF], in0=tmp_bf[oth % 2][:, 0:TF],
                        scalar1=wlnb_f[:, ot:ot + 1], scalar2=None, op0=OP.add),
                      reads=[f"tb{oth % 2}", "wlnb"], writes=[f"fo{ot}"])
                    if "ffo" in dbg:
                        dma("QG", dbg_out["ffo"][ot * 128:(ot + 1) * 128,
                                                 tf * TF:tf * TF + TF],
                            fo_t[ot][:, 0:TF], [f"fo{ot}"], ["tapffo"])

            def p5_E(tf):
                s = (tf % 2) * DB
                t0 = tf * TF
                tiles = [((lambda k=k: fo_t[k][:, 0:TF]), f"fo{k}") for k in range(DB)]
                kr, kmr = emit_ln(tiles, TF)
                for k in range(DB):
                    E("V", lambda e, k=k: e.tensor_tensor(
                        out=tmp_bf[1][:, 0:TF], in0=fo_t[k][:, 0:TF], in1=rstbc[:, 0:TF],
                        op=OP.mult),
                      reads=[f"fo{k}", kr], writes=["tb1"])
                    E("V", lambda e: e.tensor_tensor(
                        out=tmp_bf[1][:, 0:TF], in0=tmp_bf[1][:, 0:TF],
                        in1=rstbc[:, TF:2 * TF], op=OP.subtract),
                      reads=["tb1", kmr], writes=["tb1"])
                    E("V", lambda e, k=k, s=s: e.tensor_tensor(
                        out=outf[:, 0:TF], in0=x1k[s + k][:], in1=tmp_bf[1][:, 0:TF],
                        op=OP.add),
                      reads=["tb1", f"x1k{s + k}"], writes=["outf"])
                    dma("QG", out_sl[k * 128:(k + 1) * 128, t0:t0 + TF], outf[:, 0:TF],
                        ["outf"], ["OUT"])

            p5_A(0)
            for tf in range(NTF):
                p5_B(tf)
                if tf + 1 < NTF:
                    p5_A(tf + 1)
                p5_C(tf)
                p5_D(tf, 0)
                p5_D(tf, 1)
                p5_E(tf)

        _finish(rec, E, dma)
        _replay(nc, rec)
    return nc, dbg_out


def _finish(rec, E, dma):
    rec.drain("QG", "p")
    rec.drain("QS", "p")
    rec.drain("QA", "p")
    rec.drain("CC", "p")


def _replay(nc, rec):
    sems = {}
    with ExitStack() as ctx:
        pass
    sem_ctx = ExitStack()
    for e in Rec.ENG:
        sems[e] = sem_ctx.enter_context(nc.semaphore(f"sb_{e}"))
    blk_ctx = ExitStack()
    block = blk_ctx.enter_context(nc.Block())

    def replay(stream, eng_obj):
        from contextlib import ExitStack as _ES
        scope_ctx = _ES()
        for item in rec.items[stream]:
            if item[0] == "wait":
                _, other, cval = item
                eng_obj.wait_ge(sems[other], cval)
            elif item[0] == "scope":
                scope_ctx.close()
                scope_ctx = _ES()
                scope_ctx.enter_context(nc.named_scope(item[1]))
            else:
                _, fn, eng, step = item
                inst = fn(eng_obj)
                if eng is not None:
                    inst.then_inc(sems[eng], step)
        scope_ctx.close()

    @block.sync
    def _(sync):
        replay("sync", sync)

    @block.vector
    def _(v):
        replay("v", v)

    @block.scalar
    def _(s):
        replay("s", s)

    @block.tensor
    def _(t):
        replay("t", t)

    @block.gpsimd
    def _(p):
        replay("p", p)

    blk_ctx.close()
    sem_ctx.close()



def make_inputs(inputs):
    x = np.asarray(inputs["L0_attn_x"])[0]
    xT = np.ascontiguousarray(x.T.astype(np.float32))
    w_in_t = np.ascontiguousarray(np.asarray(inputs["L0_attn_W_in"]).T.astype(np.float32))
    w_x_t = np.ascontiguousarray(np.asarray(inputs["L0_attn_W_x"]).T.astype(np.float32))
    w_dt_t = np.ascontiguousarray(np.asarray(inputs["L0_attn_W_dt"]).T.astype(np.float32))
    w_out_t = np.ascontiguousarray(np.asarray(inputs["L0_attn_W_out"]).T.astype(np.float32))
    w_fc_t = np.ascontiguousarray(np.asarray(inputs["L0_ffn_W_fc"]).T.astype(np.float32))
    w_proj_t = np.ascontiguousarray(np.asarray(inputs["L0_ffn_W_proj"]).T.astype(np.float32))
    conv_w2 = np.ascontiguousarray(np.asarray(inputs["L0_attn_conv_w"])[:, 0, :].astype(np.float32))
    b_dt2 = np.ascontiguousarray(np.asarray(inputs["L0_attn_b_dt"]).astype(np.float32).reshape(D, 1))
    a_log = np.ascontiguousarray(np.asarray(inputs["L0_attn_A_log"]).astype(np.float32))
    lnw = np.ascontiguousarray(np.asarray(inputs["L0_ffn_ln_w"]).astype(np.float32).reshape(HD, 1))
    lnb = np.asarray(inputs["L0_ffn_ln_b"]).astype(ml_dtypes.bfloat16).reshape(HD, 1)
    eye = np.eye(128, dtype=ml_dtypes.bfloat16)

    in_maps = []
    for c in range(NCORES):
        t0 = c * T
        x_slc = np.zeros((D, T + HL), dtype=np.float32)
        lo = max(0, t0 - HL)
        x_slc[:, HL - (t0 - lo):] = xT[:, lo:t0 + T]
        mask = np.zeros((128, NCORES), dtype=np.float32)
        mask[:, :c] = 1.0
        im = dict(
            x_sl=x_slc, w_in_t=w_in_t, w_x_t=w_x_t, w_dt_t=w_dt_t,
            w_out_t=w_out_t, w_fc_t=w_fc_t, w_proj_t=w_proj_t,
            conv_w2=conv_w2, b_dt2=b_dt2, a_log=a_log,
            lnw_in=lnw, lnb_in=lnb, eye_in=eye,
            onescb_in=np.ones((128, 1), dtype=ml_dtypes.bfloat16),
            onesrb_in=np.ones((1, 128), dtype=ml_dtypes.bfloat16),
            onescf_in=np.ones((128, 1), dtype=np.float32),
            onesrf_in=np.ones((1, 128), dtype=np.float32),
            maskprev=mask,
            cns_in=np.array([[LNEPS, -0.5]], dtype=np.float32),
            gfc_sl=np.ascontiguousarray(w_fc_t[c * 128:(c + 1) * 128, :]),
            gpr_sl=np.ascontiguousarray(w_proj_t[c * 512:(c + 1) * 512, :]),
        )
        in_maps.append(im)
    return in_maps


def run(inputs, dbg=(), stop_after="full", p1_nchunks=NTC, dma_serial=False, **kw):
    nc, dbg_out = build(dbg=dbg, stop_after=stop_after, p1_nchunks=p1_nchunks, dma_serial=dma_serial)
    in_maps = make_inputs(inputs)
    res = run_bass_kernel_spmd(nc, in_maps, core_ids=list(range(NCORES)), **kw)
    return res


def kernel(**inputs):
    res = run(inputs)
    parts = [np.asarray(res.results[c]["out_sl"], dtype=np.float32) for c in range(NCORES)]
    full = np.concatenate([p.T for p in parts], axis=0)
    return np.ascontiguousarray(full[None, :, :], dtype=np.float32)
